# version 06
# speedup vs baseline: 1.4219x; 1.0448x over previous
import numpy as np
from contextlib import ExitStack
import concourse.bass as bass
import concourse.mybir as mybir
from concourse.bass_utils import run_bass_kernel_spmd
from concourse.alu_op_type import AluOpType as OP
from concourse.bass_primitives import MemorySpace
import ml_dtypes

F32 = mybir.dt.float32
BF16 = mybir.dt.bfloat16
I32 = mybir.dt.int32
AF = mybir.ActivationFunctionType
AX = mybir.AxisListType

D = 1024
KC = D // 128

CONSTS = [1.0 / D, 1e-5, -0.5, 0.5, 1e-20, 1e-9, 2.5, 1.25, 1.0 / 127.0,
          0.125, 8.0, 1e-24, 0.001, 0.1, 1.0, 0.0, -1e30, 3.0, -1.0, 2.0,
          1.0 / 32.0, 3.0 / 7.0, 0.7, 0.3, -8.0]


class RecIns:
    def __init__(self, cell):
        self.cell = cell

    def then_inc(self, sem, k=1):
        self.cell["incs"].append((sem, k))
        return self


class RecEng:
    def __init__(self, q, auto_sem=None):
        self._q = q
        self._auto = auto_sem

    def wait_ge(self, sem, val):
        self._q.append(("wait", sem, int(val)))

    def __getattr__(self, op):
        if op.startswith("_"):
            raise AttributeError(op)

        def call(*a, **kw):
            cell = {"op": op, "a": a, "kw": kw, "incs": []}
            if self._auto is not None:
                self._q.append(("wait", self._auto.h, self._auto.n))
                cell["incs"].append((self._auto.h, 1))
                self._auto.n += 1
            self._q.append(("op", cell))
            return RecIns(cell)
        return call


def replay(e, q):
    for entry in q:
        if entry[0] == "wait":
            e.wait_ge(entry[1], entry[2])
        else:
            cell = entry[1]
            ins = getattr(e, cell["op"])(*cell["a"], **cell["kw"])
            for sem, k in cell["incs"]:
                ins.then_inc(sem, k)


class Sem:
    def __init__(self, ctx, nc, name):
        self.h = ctx.enter_context(nc.semaphore(name))
        self.n = 0

    def inc(self, instr, k=1):
        if isinstance(instr, RecIns):
            for (sh, kk) in instr.cell["incs"]:
                if sh is self.h:
                    return self.n
        instr.then_inc(self.h, k)
        self.n += k
        return self.n


ENGS = ("sync", "gpsimd", "scalar", "tensor", "vector")


class Prog:
    def __init__(self, nc, ctx):
        self.nc = nc
        self.ctx = ctx
        self.sW = Sem(ctx, nc, "sW")
        self.sWp = [Sem(ctx, nc, "sWa"), Sem(ctx, nc, "sWb")]
        self.sXp = [Sem(ctx, nc, "sXa"), Sem(ctx, nc, "sXb")]
        self.sB = Sem(ctx, nc, "sB")
        self.sIB = Sem(ctx, nc, "sIB")
        self.bias_gate = 0
        self.yslab_gate = 0
        self.yslab_gateV = 0
        self.sI = Sem(ctx, nc, "sI")
        self.sX = Sem(ctx, nc, "sX")
        self.sT = Sem(ctx, nc, "sT")
        self.sS = Sem(ctx, nc, "sS")
        self.sV = Sem(ctx, nc, "sV")
        self.sG = Sem(ctx, nc, "sG")
        self.sO = Sem(ctx, nc, "sO")
        self.queues = {e: [] for e in ENGS}
        self.eng = {e: RecEng(self.queues[e]) for e in ENGS}
        self.eng["vector"] = RecEng(self.queues["vector"], self.sV)
        self.eng["scalar"] = RecEng(self.queues["scalar"], self.sS)
        self.NCST = len(CONSTS)
        self.c1 = ctx.enter_context(nc.sbuf_tensor("cpool1", [1, self.NCST], F32))
        self.c128 = ctx.enter_context(nc.sbuf_tensor("cpool128", [128, self.NCST], F32))
        self.cmap = {}
        self.ones_row = ctx.enter_context(nc.sbuf_tensor("ones_row", [1, 128], F32))
        self.ones_rowb = ctx.enter_context(nc.sbuf_tensor("ones_rowb", [1, 512], BF16))
        self.wbuf_free = [0, 0]
        self.gu_free = [0, 0]
        self.pd_free = [0, 0]
        self.h_prev = 0
        self.xp_cons = [0, 0]
        self.att_pq_free = [0, 0]
        self.xn_store = [0, 0]
        self.l2a_att_cons = [0, 0]
        self.l2a_ps_free = [0, 0]

    def const(self, val, wide=False):
        i = self.cmap[float(val)]
        return (self.c128 if wide else self.c1)[:, i:i + 1]

    def init_consts(self):
        if getattr(self, "_consts_done", False):
            return self.const_done
        self._consts_done = True
        v = self.eng["vector"]
        for i, val in enumerate(CONSTS):
            self.cmap[float(val)] = i
            v.memset(self.c1[:, i:i + 1], float(val))
            v.memset(self.c128[:, i:i + 1], float(val))
            self.nc.const_aps.register(F32, float(val), self.c128[:, i:i + 1])
        v.memset(self.ones_row[:], 1.0)
        ins = v.memset(self.ones_rowb[:], 1.0)
        self.sV.inc(ins)
        self.const_done = self.sV.n
        for e in ("scalar", "tensor", "gpsimd"):
            self.eng[e].wait_ge(self.sV.h, self.const_done)
        return self.const_done

    def phase_barrier(self):
        sems = {"sync": None, "gpsimd": self.sG, "scalar": self.sS,
                "tensor": self.sT, "vector": self.sV}
        for e in ENGS:
            for e2, sem in sems.items():
                if e2 == e or sem is None:
                    continue
                self.eng[e].wait_ge(sem.h, sem.n)
            for sem in (self.sX, self.sO, self.sI, self.sWp[0], self.sWp[1],
                        self.sXp[0], self.sXp[1], self.sB, self.sIB):
                self.eng[e].wait_ge(sem.h, sem.n)

    def mark_phase(self, name):
        if not hasattr(self, "phase_bounds"):
            self.phase_bounds = []
        self.phase_bounds.append((name, {e: len(self.queues[e]) for e in ENGS}))

    def _seg_replay(self, e, qname):
        q = self.queues[qname]
        bounds = getattr(self, "phase_bounds", [])
        if not bounds:
            replay(e, q)
            return
        nc = self.nc
        prev = 0
        for (name, idxs) in bounds:
            end = idxs[qname]
            if end > prev:
                with nc.named_scope(name):
                    replay(e, q[prev:end])
                prev = end
        replay(e, q[prev:])

    def finalize(self, block):
        qs = self.queues
        block.sync(lambda e: self._seg_replay(e, "sync"))
        block.gpsimd(lambda e: self._seg_replay(e, "gpsimd"))
        block.scalar(lambda e: self._seg_replay(e, "scalar"))
        block.tensor(lambda e: self._seg_replay(e, "tensor"))
        block.vector(lambda e: self._seg_replay(e, "vector"))


def emit_ln(P, xh_v, ln_v, Th, sc, affine=None):
    s, t, v = P.eng["scalar"], P.eng["tensor"], P.eng["vector"]
    sq, pM, pQ, pB, pB2 = sc["sq"], sc["pM"], sc["pQ"], sc["pB"], sc["pB2"]
    rows, rbB, mbB, onecol, tmp = sc["rows"], sc["rbB"], sc["mbB"], sc["onecol"], sc["tmp"]
    assert Th <= 512
    t.wait_ge(P.sV.h, P.sV.n)
    t.wait_ge(P.sS.h, P.sS.n)
    s.wait_ge(P.sV.h, P.sV.n)
    distinct_sq = len({id(x.tensor if hasattr(x, "tensor") else x) for x in sq}) >= KC
    onec_in = onecol[:] if xh_v[0].tensor.dtype == F32 else sc["onecolb"][:]
    if distinct_sq:
        for k in range(KC):
            ins = s.square(out=sq[k][:, :Th], in_=xh_v[k])
        P.sS.inc(ins)
        t.wait_ge(P.sS.h, P.sS.n)
        for k in range(KC):
            t.matmul(pM[0:1, :Th], onec_in, xh_v[k], start=(k == 0), stop=(k == KC - 1))
        for k in range(KC):
            ins = t.matmul(pQ[0:1, :Th], sc["onecolb"][:], sq[k][:, :Th], start=(k == 0), stop=(k == KC - 1))
        P.sT.inc(ins)
    else:
        for k in range(KC):
            t.matmul(pM[0:1, :Th], onec_in, xh_v[k], start=(k == 0), stop=(k == KC - 1))
        for k in range(KC):
            if k >= 2:
                s.wait_ge(P.sT.h, P.sT.n)
            ins = s.square(out=sq[k % 2][:, :Th], in_=xh_v[k])
            P.sS.inc(ins)
            t.wait_ge(P.sS.h, P.sS.n)
            ins = t.matmul(pQ[0:1, :Th], sc["onecolb"][:], sq[k % 2][:, :Th], start=(k == 0), stop=(k == KC - 1))
            P.sT.inc(ins)
    mrow = rows[:, 0:Th]
    m2row = rows[:, 512:512 + Th]
    vrow = rows[:, 1024:1024 + Th]
    rrow = vrow
    s.wait_ge(P.sT.h, P.sT.n)
    s.activation(out=mrow, in_=pM[0:1, :Th], func=AF.Identity, scale=1.0 / D)
    ins = s.activation(out=m2row, in_=pM[0:1, :Th], func=AF.Square, scale=1.0 / D)
    P.sS.inc(ins)
    v.wait_ge(P.sS.h, P.sS.n)
    ins = v.scalar_tensor_tensor(out=vrow, in0=pQ[0:1, :Th], scalar=1.0 / D,
                                 in1=m2row, op0=OP.mult, op1=OP.subtract)
    P.sV.inc(ins)
    s.wait_ge(P.sV.h, P.sV.n)
    ins = s.activation(out=m2row, in_=vrow, func=AF.Ln, bias=1e-5)
    P.sS.inc(ins)
    s.wait_ge(P.sS.h, P.sS.n)
    ins = s.activation(out=rrow, in_=m2row, func=AF.Exp, scale=-0.5)
    P.sS.inc(ins)
    v.wait_ge(P.sS.h, P.sS.n)
    ins = v.tensor_tensor(out=mrow, in0=mrow, in1=rrow, op=OP.mult)
    P.sV.inc(ins)
    t.wait_ge(P.sV.h, P.sV.n)
    t.matmul(pB[:, :Th], P.ones_row[0:1, 0:128], rrow, start=True, stop=True)
    ins = t.matmul(pB2[:, :Th], P.ones_row[0:1, 0:128], mrow, start=True, stop=True)
    P.sT.inc(ins)
    s.wait_ge(P.sT.h, P.sT.n)
    s.copy(out=rbB[:, :Th], in_=pB[:, :Th])
    ins = s.copy(out=mbB[:, :Th], in_=pB2[:, :Th])
    P.sS.inc(ins)
    v.wait_ge(P.sS.h, P.sS.n)
    for k in range(KC):
        v.tensor_tensor(out=tmp[:, :Th], in0=xh_v[k], in1=rbB[:, :Th], op=OP.mult)
        if affine is None:
            ins = v.tensor_tensor(out=ln_v[k], in0=tmp[:, :Th], in1=mbB[:, :Th], op=OP.subtract)
        else:
            v.tensor_tensor(out=tmp[:, :Th], in0=tmp[:, :Th], in1=mbB[:, :Th], op=OP.subtract)
            g_col, b_col = affine
            ins = v.tensor_scalar(out=ln_v[k], in0=tmp[:, :Th],
                                  scalar1=g_col[k], scalar2=b_col[k],
                                  op0=OP.mult, op1=OP.add)
    P.sV.inc(ins)
    return P.sV.n


def router_softmax_topk(P, cfg, lnh, Th, sc, ln_done):
    s, t, v, sy = P.eng["scalar"], P.eng["tensor"], P.eng["vector"], P.eng["sync"]
    E, topk, coef, renorm_eps = cfg["E"], cfg["topk"], cfg["coef"], cfg.get("renorm_eps")
    pl, pt = sc["pl"], sc["pt"]
    rw = cfg["rw_tile"]
    rs, r2, r3, rcol, combT, identb = sc["rs"], sc["r2"], sc["r3"], sc["rcol"], sc["combT"], sc["identb"]
    t.wait_ge(P.sV.h, ln_done)
    for ti in range(Th // 128):
        t.wait_ge(P.sS.h, P.sS.n)
        t.wait_ge(P.sV.h, P.sV.n)
        for k in range(KC):
            ins = t.matmul(pl[:, :E], lnh[k][:, ti * 128:(ti + 1) * 128],
                           rw[:, k * E:(k + 1) * E], start=(k == 0), stop=(k == KC - 1))
        P.sT.inc(ins)
        v.wait_ge(P.sT.h, P.sT.n)
        v.tensor_reduce(out=rcol[:, 0:1], in_=pl[:, :E], axis=AX.X, op=OP.max, negate=True)
        ins = v.tensor_copy(out=rs[:, :E], in_=pl[:, :E])
        P.sV.inc(ins)
        s.wait_ge(P.sV.h, P.sV.n)
        ins = s.activation(out=rs[:, :E], in_=rs[:, :E], func=AF.Exp,
                           bias=rcol[:, 0:1], accum_out=rcol[:, 1:2])
        P.sS.inc(ins)
        v.wait_ge(P.sS.h, P.sS.n)
        v.reciprocal(out=rcol[:, 1:2], in_=rcol[:, 1:2])
        v.tensor_scalar(out=rs[:, :E], in0=rs[:, :E], scalar1=rcol[:, 1:2],
                        scalar2=None, op0=OP.mult)
        if E < 8:
            v.memset(rs[:, E:8], -1e30)
        v.max(out=r3[:, 0:8], in_=rs[:, 0:max(E, 8)])
        v.tensor_scalar(out=r2[:, :E], in0=rs[:, :E], scalar1=r3[:, topk - 1:topk],
                        scalar2=None, op0=OP.is_ge)
        v.tensor_tensor(out=r2[:, :E], in0=r2[:, :E], in1=rs[:, :E], op=OP.mult)
        if renorm_eps is not None:
            v.tensor_reduce(out=rcol[:, 2:3], in_=r2[:, :E], axis=AX.X, op=OP.add)
            v.tensor_scalar(out=rcol[:, 2:3], in0=rcol[:, 2:3], scalar1=float(renorm_eps),
                            scalar2=None, op0=OP.add)
            v.reciprocal(out=rcol[:, 3:4], in_=rcol[:, 2:3])
            v.tensor_scalar(out=rcol[:, 3:4], in0=rcol[:, 3:4], scalar1=float(coef),
                            scalar2=None, op0=OP.mult)
            ins = v.tensor_scalar(out=r2[:, :E], in0=r2[:, :E], scalar1=rcol[:, 3:4],
                                  scalar2=None, op0=OP.mult)
        else:
            ins = v.tensor_scalar(out=r2[:, :E], in0=r2[:, :E], scalar1=float(coef),
                                  scalar2=None, op0=OP.mult)
        P.sV.inc(ins)
        t.wait_ge(P.sV.h, P.sV.n)
        ins = t.transpose(pt[:E, 0:128], r2[:, :E], identb[:])
        P.sT.inc(ins)
        s.wait_ge(P.sT.h, P.sT.n)
        ins = s.copy(out=combT[:E, ti * 128:(ti + 1) * 128], in_=pt[:E, 0:128])
        P.sS.inc(ins)
    return P.sS.n


def router_l1_group(P, cfg, lnh, Th, sc, ln_done):
    s, t, v, sy = P.eng["scalar"], P.eng["tensor"], P.eng["vector"], P.eng["sync"]
    E = 16
    pl, pt = sc["pl"], sc["pt"]
    rw = cfg["rw_tile"]
    rs, r2, r3, rcol, combT, identb = sc["rs"], sc["r2"], sc["r3"], sc["rcol"], sc["combT"], sc["identb"]
    rx = sc["rx"]
    ry = sc["ry"]
    ebB = sc["ebB"]
    onesr = sc["ones16"]
    t.wait_ge(P.sS.h, P.sS.n)
    t.wait_ge(P.sV.h, P.sV.n)
    ins = t.matmul(pt[0:128, 0:16], P.ones_row[0:1, 0:128], sc["ebrow"][0:1, 0:16],
                   start=True, stop=True)
    P.sT.inc(ins)
    s.wait_ge(P.sT.h, P.sT.n)
    ins = s.copy(out=ebB[:], in_=pt[0:128, 0:16])
    P.sS.inc(ins)
    eb_done = P.sS.n
    t.wait_ge(P.sV.h, ln_done)
    for ti in range(Th // 128):
        t.wait_ge(P.sS.h, P.sS.n)
        t.wait_ge(P.sV.h, P.sV.n)
        for k in range(KC):
            ins = t.matmul(pl[:, :E], lnh[k][:, ti * 128:(ti + 1) * 128],
                           rw[:, k * E:(k + 1) * E], start=(k == 0), stop=(k == KC - 1))
        P.sT.inc(ins)
        s.wait_ge(P.sT.h, P.sT.n)
        s.wait_ge(P.sS.h, eb_done)
        ins = s.activation(out=rs[:, :E], in_=pl[:, :E], func=AF.Sigmoid)
        P.sS.inc(ins)
        v.wait_ge(P.sS.h, P.sS.n)
        v.tensor_tensor(out=r2[:, :E], in0=rs[:, :E], in1=ebB[:, :E], op=OP.add)
        g4 = [r2[:, :E].rearrange("p (g j) -> p g j", j=4)[:, :, j] for j in range(4)]
        pairs = [(0, 1), (0, 2), (0, 3), (1, 2), (1, 3), (2, 3)]
        first = True
        for (j1, j2) in pairs:
            v.tensor_tensor(out=ry[:, 0:4], in0=g4[j1], in1=g4[j2], op=OP.add)
            if first:
                v.tensor_copy(out=rx[:, 0:4], in_=ry[:, 0:4])
                first = False
            else:
                v.tensor_tensor(out=rx[:, 0:4], in0=rx[:, 0:4], in1=ry[:, 0:4], op=OP.max)
        gs = [rx[:, j:j + 1] for j in range(4)]
        first = True
        for (j1, j2) in pairs:
            v.tensor_tensor(out=ry[:, 6:7], in0=gs[j1], in1=gs[j2], op=OP.min)
            if first:
                v.tensor_copy(out=ry[:, 7:8], in_=ry[:, 6:7])
                first = False
            else:
                v.tensor_tensor(out=ry[:, 7:8], in0=ry[:, 7:8], in1=ry[:, 6:7], op=OP.max)
        v.tensor_scalar(out=rx[:, 0:4], in0=rx[:, 0:4], scalar1=ry[:, 7:8],
                        scalar2=None, op0=OP.is_ge)
        rsm = sc["rsm"]
        smv = rsm[:, :E].rearrange("p (g j) -> p g j", j=4)
        for j in range(4):
            v.tensor_copy(out=smv[:, :, j], in_=rx[:, 0:4])
        v.tensor_tensor(out=ry[:, :E], in0=r2[:, :E], in1=rsm[:, :E], op=OP.mult)
        v.tensor_scalar(out=rx[:, :E], in0=ry[:, :E], scalar1=0.0, scalar2=None, op0=OP.max)
        v.max(out=r3[:, 0:8], in_=rx[:, :E])
        v.tensor_scalar(out=ry[:, :E], in0=rx[:, :E], scalar1=r3[:, 3:4],
                        scalar2=None, op0=OP.is_ge)
        v.tensor_scalar(out=rx[:, :E], in0=rx[:, :E], scalar1=0.0, scalar2=None, op0=OP.is_gt)
        v.tensor_tensor(out=ry[:, :E], in0=ry[:, :E], in1=rx[:, :E], op=OP.mult)
        v.tensor_reduce(out=rcol[:, 0:1], in_=ry[:, :E], axis=AX.X, op=OP.add)
        v.tensor_scalar(out=rcol[:, 0:1], in0=rcol[:, 0:1], scalar1=-1.0, scalar2=4.0,
                        op0=OP.mult, op1=OP.add)
        v.tensor_scalar(out=rx[:, :E], in0=rsm[:, :E], scalar1=0.5, scalar2=None, op0=OP.is_lt)
        v.tensor_tensor_scan(out=rsm[:, :E], data0=onesr[:, :E], data1=rx[:, :E],
                             initial=0.0, op0=OP.mult, op1=OP.add)
        v.tensor_scalar(out=rsm[:, :E], in0=rsm[:, :E], scalar1=rcol[:, 0:1],
                        scalar2=None, op0=OP.is_le)
        v.tensor_tensor(out=rx[:, :E], in0=rx[:, :E], in1=rsm[:, :E], op=OP.mult)
        v.tensor_tensor(out=ry[:, :E], in0=ry[:, :E], in1=rx[:, :E], op=OP.add)
        v.tensor_tensor(out=ry[:, :E], in0=ry[:, :E], in1=rs[:, :E], op=OP.mult)
        v.tensor_reduce(out=rcol[:, 1:2], in_=ry[:, :E], axis=AX.X, op=OP.add)
        v.tensor_scalar(out=rcol[:, 1:2], in0=rcol[:, 1:2], scalar1=1e-20, scalar2=None, op0=OP.add)
        v.reciprocal(out=rcol[:, 2:3], in_=rcol[:, 1:2])
        v.tensor_scalar(out=rcol[:, 2:3], in0=rcol[:, 2:3], scalar1=float(cfg["coef"]),
                        scalar2=None, op0=OP.mult)
        ins = v.tensor_scalar(out=ry[:, :E], in0=ry[:, :E], scalar1=rcol[:, 2:3],
                              scalar2=None, op0=OP.mult)
        P.sV.inc(ins)
        t.wait_ge(P.sV.h, P.sV.n)
        ins = t.transpose(pt[:E, 0:128], ry[:, :E], identb[:])
        P.sT.inc(ins)
        s.wait_ge(P.sT.h, P.sT.n)
        ins = s.copy(out=combT[:E, ti * 128:(ti + 1) * 128], in_=pt[:E, 0:128])
        P.sS.inc(ins)
    return P.sS.n


def emit_ffn_half(P, cfg, xh, Th, wdram, sc):
    s, t, v, g, sy = (P.eng[e] for e in ("scalar", "tensor", "vector", "gpsimd", "sync"))
    lnh = sc["lnh"]
    NT = Th // 512

    ln_done = 0
    for nb in range(NT):
        xh_v = [xx[:, nb * 512:(nb + 1) * 512] for xx in xh]
        ln_v = [xx[:, nb * 512:(nb + 1) * 512] for xx in lnh]
        ln_done = emit_ln(P, xh_v, ln_v, 512, sc)

    combT = None
    comb_done = 0
    if cfg.get("router") is not None:
        ins = v.memset(sc["combT"][:, :], 0.0)
        P.sV.inc(ins)
        comb_done = cfg["router"](P, cfg["router_cfg"], lnh, Th, sc, ln_done)
        combT = sc["combT"]
        t.wait_ge(P.sS.h, comb_done)

    pg, pu, pd = sc["pg"], sc["pu"], sc["pd"]
    wg, wu, wd = sc["wg"], sc["wu"], sc["wd"]
    sg_t, ug_t, hslab, cbB = sc["sg"], sc["ug"], sc["h"], sc["cbB"]
    pC = sc["pB"]

    experts = []
    if cfg.get("dff_sh", 0):
        experts.append(("sh", cfg["dff_sh"]))
    for e in range(cfg.get("n_exp", 0)):
        experts.append((e, cfg["dff"]))

    wbuf_free = P.wbuf_free
    gu_free = P.gu_free
    pd_free = P.pd_free
    slab_ctr = 0

    for (e, dff) in experts:
        nslab = dff // 512
        if e != "sh" and combT is not None:
            ins = v.tensor_scalar(out=sc["sel"][:, :], in0=sc["ids16"][:, :],
                                  scalar1=float(e), scalar2=None, op0=OP.is_equal)
            P.sV.inc(ins)
            t.wait_ge(P.sV.h, P.sV.n)
            t.wait_ge(P.sS.h, P.sS.n)
            for nb in range(NT):
                t.wait_ge(P.sS.h, P.sS.n)
                ins = t.matmul(pC[:, 0:512], sc["sel"][:, :],
                               combT[0:16, nb * 512:(nb + 1) * 512], start=True, stop=True)
                P.sT.inc(ins)
                s.wait_ge(P.sT.h, P.sT.n)
                ins = s.copy(out=cbB[:, nb * 512:(nb + 1) * 512], in_=pC[:, 0:512])
                P.sS.inc(ins)
            cb_done = P.sS.n
        else:
            cb_done = None

        for sl in range(nslab):
            wpar = slab_ctr % 2
            slab_ctr += 1
            if e == "sh":
                gsrc, usrc, dsrc = wdram["sh_gT"], wdram["sh_uT"], wdram["sh_dT"]
            else:
                gsrc, usrc, dsrc = wdram["gT"][e], wdram["uT"][e], wdram["dT"][e]
            sw = P.sWp[wpar]
            g.wait_ge(P.sT.h, wbuf_free[wpar])
            ins = g.dma_start(out=wg[wpar][:].rearrange("p (k m) -> p k m", k=KC),
                              in_=gsrc[:, sl * 512:(sl + 1) * 512].rearrange("(k p) m -> p k m", p=128))
            sw.inc(ins, 16)
            ins = g.dma_start(out=wu[wpar][:].rearrange("p (k m) -> p k m", k=KC),
                              in_=usrc[:, sl * 512:(sl + 1) * 512].rearrange("(k p) m -> p k m", p=128))
            sw.inc(ins, 16)
            ins = g.dma_start(out=wd[wpar][:].rearrange("p (k m) -> p k m", k=4),
                              in_=dsrc[sl * 512:(sl + 1) * 512, :].rearrange("(k p) m -> p k m", p=128))
            sw.inc(ins, 16)
            w_ready = sw.n

            t.wait_ge(sw.h, w_ready)
            t.wait_ge(P.sV.h, ln_done)
            if cb_done is not None:
                t.wait_ge(P.sS.h, cb_done)
            for m in range(4):
                for nb in range(NT):
                    par = (m * NT + nb) % 2
                    t.wait_ge(P.sV.h, gu_free[par])
                    for k in range(KC):
                        t.matmul(pg[par][:, 0:512], wg[wpar][:, 512 * k + 128 * m:512 * k + 128 * m + 128],
                                 lnh[k][:, nb * 512:(nb + 1) * 512], start=(k == 0), stop=(k == KC - 1))
                    for k in range(KC):
                        ins = t.matmul(pu[par][:, 0:512], wu[wpar][:, 512 * k + 128 * m:512 * k + 128 * m + 128],
                                       lnh[k][:, nb * 512:(nb + 1) * 512], start=(k == 0), stop=(k == KC - 1))
                    P.sT.inc(ins)
                    s.wait_ge(P.sT.h, P.sT.n)
                    s.wait_ge(P.sV.h, P.h_prev)
                    act_kw = {}
                    if cfg.get("act_scale"):
                        act_kw["scale"] = float(cfg["act_scale"])
                    ins = s.activation(out=sg_t[:, 0:512], in_=pg[par][:, 0:512], func=AF.Silu, **act_kw)
                    P.sS.inc(ins)
                    v.wait_ge(P.sS.h, P.sS.n)
                    if e == "sh" or combT is None:
                        coef = cfg["coef_shared"] * (cfg.get("act_scale") or 1.0)
                        v.tensor_scalar(out=ug_t[:, 0:512], in0=pu[par][:, 0:512],
                                        scalar1=float(coef), scalar2=None, op0=OP.mult)
                    else:
                        v.tensor_tensor(out=ug_t[:, 0:512], in0=pu[par][:, 0:512],
                                        in1=cbB[:, nb * 512:(nb + 1) * 512], op=OP.mult)
                    ins = v.tensor_tensor(out=hslab[m][:, nb * 512:(nb + 1) * 512],
                                          in0=ug_t[:, 0:512], in1=sg_t[:, 0:512], op=OP.mult)
                    P.sV.inc(ins)
                    gu_free[par] = P.sV.n
                    P.h_prev = P.sV.n
            h_done = P.sV.n

            t.wait_ge(P.sV.h, h_done)
            for f in range(KC):
                for nb in range(NT):
                    par = (f * NT + nb) % 2
                    t.wait_ge(P.sV.h, pd_free[par])
                    for k in range(4):
                        ins = t.matmul(pd[par][:, 0:512], wd[wpar][:, 1024 * k + 128 * f:1024 * k + 128 * f + 128],
                                       hslab[k][:, nb * 512:(nb + 1) * 512], start=(k == 0), stop=(k == 3))
                    P.sT.inc(ins)
                    v.wait_ge(P.sT.h, P.sT.n)
                    ins = v.tensor_tensor(out=xh[f][:, nb * 512:(nb + 1) * 512],
                                          in0=pd[par][:, 0:512],
                                          in1=xh[f][:, nb * 512:(nb + 1) * 512], op=OP.add)
                    P.sV.inc(ins)
                    pd_free[par] = P.sV.n
            wbuf_free[wpar] = P.sT.n


def load_xh(P, xdram, xh, t0, Th):
    sy = P.eng["sync"]
    sy.wait_ge(P.sO.h, P.sO.n)
    for k in range(KC):
        ins = sy.dma_start(out=xh[k][:, :Th], in_=xdram[k * 128:(k + 1) * 128, t0:t0 + Th])
        P.sX.inc(ins, 16)
    for e in ("scalar", "tensor", "vector"):
        P.eng[e].wait_ge(P.sX.h, P.sX.n)


def store_xh(P, xdram, xh, t0, Th):
    sy = P.eng["sync"]
    sy.wait_ge(P.sV.h, P.sV.n)
    for k in range(KC):
        ins = sy.dma_start(out=xdram[k * 128:(k + 1) * 128, t0:t0 + Th], in_=xh[k][:, :Th])
        P.sO.inc(ins, 16)


def emit_ffn_block(P, name, xdram, wdram, T, sc):
    Th = min(T, 2048)
    xh = sc["xh"]
    if name == "l0f":
        cfg = dict(n_exp=8, dff=512, dff_sh=1024, coef_shared=0.5,
                   router=router_softmax_topk,
                   router_cfg=dict(E=8, topk=2, coef=0.5, renorm_eps=None,
                                   rw_tile=sc["rw_l0"]))
        wd = dict(gT=wdram["l0f_gT"], uT=wdram["l0f_uT"], dT=wdram["l0f_dT"],
                  sh_gT=wdram["l0f_shgT"], sh_uT=wdram["l0f_shuT"], sh_dT=wdram["l0f_shdT"])
    elif name == "l1f":
        cfg = dict(n_exp=16, dff=512, dff_sh=1024, coef_shared=0.5,
                   router=router_l1_group,
                   router_cfg=dict(coef=2.5 * 0.5, rw_tile=sc["rw_l1"]))
        wd = dict(gT=wdram["l1f_gT"], uT=wdram["l1f_uT"], dT=wdram["l1f_dT"],
                  sh_gT=wdram["l1f_shgT"], sh_uT=wdram["l1f_shuT"], sh_dT=wdram["l1f_shdT"])
    elif name == "l2f":
        cfg = dict(n_exp=7, dff=4096, dff_sh=4096, coef_shared=0.5,
                   router=router_softmax_topk,
                   router_cfg=dict(E=7, topk=2, coef=0.5, renorm_eps=1e-9,
                                   rw_tile=sc["rw_l2"]))
        wd = dict(gT=wdram["l2f_gT"], uT=wdram["l2f_uT"], dT=wdram["l2f_dT"],
                  sh_gT=wdram["l2f_shgT"], sh_uT=wdram["l2f_shuT"], sh_dT=wdram["l2f_shdT"])
    elif name == "l3f":
        cfg = dict(n_exp=0, dff=0, dff_sh=3072, router=None,
                   coef_shared=0.5 / 127.0, act_scale=1.0 / 127.0)
        wd = dict(sh_gT=wdram["l3f_gT"], sh_uT=wdram["l3f_uT"], sh_dT=wdram["l3f_dT"])
    for hf in range(T // Th):
        t0 = hf * Th
        load_xh(P, xdram, xh, t0, Th)
        emit_ffn_half(P, cfg, xh, Th, wd, sc)
        store_xh(P, xdram, xh, t0, Th)


def alloc_ffn_scratch(P, ctx, Th):
    nc = P.nc
    sc = {}
    sc["xh"] = [ctx.enter_context(nc.sbuf_tensor(f"xh{k}", [128, Th], F32)) for k in range(KC)]
    sc["lnh"] = [ctx.enter_context(nc.sbuf_tensor(f"lnh{k}", [128, Th], BF16)) for k in range(KC)]
    sc["sq"] = [ctx.enter_context(nc.sbuf_tensor(f"sq{k}", [128, 512], BF16)) for k in range(KC)]
    sc["rows"] = ctx.enter_context(nc.sbuf_tensor("rows", [1, 3 * 512], F32))
    sc["rbB"] = ctx.enter_context(nc.sbuf_tensor("rbB", [128, 512], F32))
    sc["mbB"] = ctx.enter_context(nc.sbuf_tensor("mbB", [128, 512], F32))
    sc["tmp"] = ctx.enter_context(nc.sbuf_tensor("lntmp", [128, 512], F32))
    sc["onecol"] = ctx.enter_context(nc.sbuf_tensor("onecol", [128, 1], F32))
    sc["onecolb"] = ctx.enter_context(nc.sbuf_tensor("onecolb", [128, 1], BF16))
    sc["wg"] = [ctx.enter_context(nc.sbuf_tensor(f"wg{i}", [128, 4096], BF16)) for i in range(2)]
    sc["wu"] = [ctx.enter_context(nc.sbuf_tensor(f"wu{i}", [128, 4096], BF16)) for i in range(2)]
    sc["wd"] = [ctx.enter_context(nc.sbuf_tensor(f"wd{i}", [128, 4096], BF16)) for i in range(2)]
    sc["sg"] = ctx.enter_context(nc.sbuf_tensor("sgt", [128, 512], BF16))
    sc["ug"] = ctx.enter_context(nc.sbuf_tensor("ugt", [128, 512], BF16))
    sc["h"] = [ctx.enter_context(nc.sbuf_tensor(f"hs{m}", [128, Th], BF16)) for m in range(4)]
    sc["cbB"] = ctx.enter_context(nc.sbuf_tensor("cbB", [128, Th], F32))
    sc["rwf"] = ctx.enter_context(nc.sbuf_tensor("rwf", [128, 8 * 16], F32))
    sc["rw_l0"] = ctx.enter_context(nc.sbuf_tensor("rw_l0", [128, 8 * 8], BF16))
    sc["rw_l1"] = ctx.enter_context(nc.sbuf_tensor("rw_l1", [128, 8 * 16], BF16))
    sc["rw_l2"] = ctx.enter_context(nc.sbuf_tensor("rw_l2", [128, 8 * 7], BF16))
    sc["rs"] = ctx.enter_context(nc.sbuf_tensor("rs", [128, 16], F32))
    sc["r2"] = ctx.enter_context(nc.sbuf_tensor("r2", [128, 16], F32))
    sc["r3"] = ctx.enter_context(nc.sbuf_tensor("r3", [128, 8], F32))
    sc["rx"] = ctx.enter_context(nc.sbuf_tensor("rx", [128, 16], F32))
    sc["ry"] = ctx.enter_context(nc.sbuf_tensor("ry", [128, 16], F32))
    sc["rsm"] = ctx.enter_context(nc.sbuf_tensor("rsm", [128, 16], F32))
    sc["ebB"] = ctx.enter_context(nc.sbuf_tensor("ebB", [128, 16], F32))
    sc["ebrow"] = ctx.enter_context(nc.sbuf_tensor("ebrow", [1, 16], F32))
    sc["ones16"] = ctx.enter_context(nc.sbuf_tensor("ones16", [128, 16], F32))
    sc["rcol"] = ctx.enter_context(nc.sbuf_tensor("rcol", [128, 8], F32))
    sc["combT"] = ctx.enter_context(nc.sbuf_tensor("combT", [16, Th], F32))
    sc["ids16"] = ctx.enter_context(nc.sbuf_tensor("ids16", [16, 128], F32))
    sc["sel"] = ctx.enter_context(nc.sbuf_tensor("sel", [16, 128], F32))
    sc["identb"] = ctx.enter_context(nc.sbuf_tensor("identb", [128, 128], F32))
    sc["pg"] = [ctx.enter_context(nc.psum_tensor(f"pg{i}", [128, 512], F32)) for i in range(2)]
    sc["pu"] = [ctx.enter_context(nc.psum_tensor(f"pu{i}", [128, 512], F32)) for i in range(2)]
    sc["pd"] = [ctx.enter_context(nc.psum_tensor(f"pd{i}", [128, 512], F32)) for i in range(2)]
    sc["pM"] = sc["pd"][0]
    sc["pQ"] = sc["pd"][1]
    sc["pB"] = sc["pg"][0]
    sc["pB2"] = sc["pu"][0]
    sc["pl"] = sc["pg"][1]
    sc["pt"] = sc["pu"][1]
    return sc


def common_init(P, sc, ident_in, startup_loads=(), router_items=()):
    sy, v = P.eng["sync"], P.eng["vector"]
    P.init_consts()
    v.memset(sc["onecol"][:], 1.0)
    ins = v.memset(sc["onecolb"][:], 1.0)
    if "ones16" in sc:
        ins = v.memset(sc["ones16"][:], 1.0)
    P.sV.inc(ins)
    if "identb" in sc:
        ins = sy.dma_start(out=sc["identb"][:], in_=ident_in)
        P.sI.inc(ins, 16)
    if "identbb" in sc and "identb" not in sc:
        gq = P.eng["gpsimd"]
        ins = gq.dma_start(out=sc["identbb"][:, :], in_=ident_in)
        P.sIB.inc(ins, 16)
        for e in ("tensor",):
            P.eng[e].wait_ge(P.sIB.h, P.sIB.n)
    for (dst, src) in startup_loads:
        ins = sy.dma_start(out=dst, in_=src)
        P.sI.inc(ins, 16)
    g = P.eng["gpsimd"]
    if "ids16" in sc:
        ins = g.iota(sc["ids16"][:, :], pattern=[[0, 128]], base=0, channel_multiplier=1,
                     allow_small_or_imprecise_dtypes=True)
        P.sG.inc(ins)
        P.eng["vector"].wait_ge(P.sG.h, P.sG.n)

    off = 0
    metas = []
    for (dram, key, E) in router_items:
        ins = sy.dma_start(out=sc["rwf"][:, off:off + KC * E].rearrange("p (k e) -> p k e", k=KC),
                           in_=dram.rearrange("(k p) e -> p k e", p=128))
        P.sI.inc(ins, 16)
        metas.append((key, off, E))
        off += KC * E
    for e in ("scalar", "tensor", "vector", "gpsimd"):
        P.eng[e].wait_ge(P.sI.h, P.sI.n)
    v = P.eng["vector"]
    if "identbb" in sc and "identb" in sc:
        ins = v.tensor_copy(out=sc["identbb"][:, :], in_=sc["identb"][:, :])
        P.sV.inc(ins)
    for (key, o, E) in metas:
        ins = v.tensor_copy(out=sc[key][:, :KC * E], in_=sc["rwf"][:, o:o + KC * E])
        P.sV.inc(ins)
    P.eng["tensor"].wait_ge(P.sV.h, P.sV.n)
    P.startup_done = P.sI.n




def alloc_attn_scratch(P, ctx, T):
    nc = P.nc
    sc = {}
    sc["onecol"] = ctx.enter_context(nc.sbuf_tensor("aonec", [128, 1], F32))
    sc["onecolb"] = ctx.enter_context(nc.sbuf_tensor("aonecb", [128, 1], BF16))
    sc["xslab"] = [ctx.enter_context(nc.sbuf_tensor(f"axs{k}", [128, 512], F32)) for k in range(KC)]
    sc["lnf"] = [ctx.enter_context(nc.sbuf_tensor(f"alnf{k}", [128, T], BF16)) for k in range(KC)]
    sc["otF"] = [ctx.enter_context(nc.sbuf_tensor(f"aotF{k}", [128, T], BF16)) for k in range(KC)]
    sc["qT"] = ctx.enter_context(nc.sbuf_tensor("aqT", [128, T], BF16))
    sc["kT"] = ctx.enter_context(nc.sbuf_tensor("akT", [128, T], BF16))
    sc["vtm"] = ctx.enter_context(nc.sbuf_tensor("avtm", [128, T], BF16))
    sc["wqkv"] = [ctx.enter_context(nc.sbuf_tensor(f"awq{i}", [128, 8 * 384], BF16)) for i in range(2)]
    sc["wpro"] = ctx.enter_context(nc.sbuf_tensor("awpro", [128, 8 * 256], BF16))
    sc["bias"] = ctx.enter_context(nc.sbuf_tensor("abias", [1, 1024], BF16))
    sc["Pt"] = [ctx.enter_context(nc.sbuf_tensor(f"aPt{i}", [128, 512], BF16)) for i in range(2)]
    sc["PTs"] = [ctx.enter_context(nc.sbuf_tensor(f"aPTs{i}", [128, 512], BF16)) for i in range(2)]

    sc["Ob"] = None

    sc["maskM"] = ctx.enter_context(nc.sbuf_tensor("amaskM", [128, 896], BF16))
    sc["maskv"] = [sc["maskM"][:, 384 - 128 * a:896 - 128 * a] for a in range(4)]
    sc["iotam"] = ctx.enter_context(nc.sbuf_tensor("aiotam", [128, 512], F32))
    sc["sqn"] = sc["iotam"]
    sc["tmp"] = sc["iotam"]
    sc["rowsA"] = None
    sc["xbuf"] = [None, None]
    sc["xnew"] = [None, None]
    sc["sq"] = [sc["Pt"][0], sc["PTs"][0]] * 4
    sc["rows"] = ctx.enter_context(nc.sbuf_tensor("alrows", [1, 3 * 512], F32))
    sc["rbB"] = ctx.enter_context(nc.sbuf_tensor("arbB", [128, 512], F32))
    sc["Ob"] = sc["rbB"].bitcast(BF16)[:, 0:64]
    sc["mbB"] = ctx.enter_context(nc.sbuf_tensor("ambB", [128, 512], F32))
    sc["identbb"] = ctx.enter_context(nc.sbuf_tensor("aidentbb", [128, 128], BF16))
    sc["pQK"] = [ctx.enter_context(nc.psum_tensor(f"apqk{i}", [128, 512], F32)) for i in range(2)]
    sc["pPT"] = ctx.enter_context(nc.psum_tensor("apPT", [128, 512], BF16))
    sc["pO"] = ctx.enter_context(nc.psum_tensor("apO", [128, 64], F32))
    sc["pR"] = ctx.enter_context(nc.psum_tensor("apR", [128, 512], F32))
    sc["pS"] = [ctx.enter_context(nc.psum_tensor("apS", [128, 512], F32)), sc["pR"]]
    sc["pM"] = ctx.enter_context(nc.psum_tensor("apM", [1, 512], F32))
    sc["pQ"] = ctx.enter_context(nc.psum_tensor("apQs", [1, 512], F32))
    sc["pB"] = sc["pQK"][0]
    sc["pB2"] = sc["pQK"][1]
    return sc


def emit_attn_block(P, which, xdram, wdram, T, sc):
    s, t, v, g, sy = (P.eng[e] for e in ("scalar", "tensor", "vector", "gpsimd", "sync"))
    sc["xbuf"] = [sc["xslab"][0], sc["xslab"][1]]
    sc["xnew"] = [sc["xslab"][2], sc["xslab"][3]]
    NT = T // 512
    l3 = (which == "l3a")

    ins = g.iota(sc["maskM"][:, :], pattern=[[1, 896]], base=-384, channel_multiplier=-1,
                 allow_small_or_imprecise_dtypes=True)
    P.sG.inc(ins)
    v.wait_ge(P.sG.h, P.sG.n)
    v.tensor_scalar(out=sc["maskM"][:, :], in0=sc["maskM"][:, :],
                    scalar1=0.0, scalar2=None, op0=OP.is_le)
    ins = v.tensor_scalar(out=sc["maskM"][:, :], in0=sc["maskM"][:, :],
                          scalar1=1e31, scalar2=-1e31, op0=OP.mult, op1=OP.add)
    P.sV.inc(ins)
    mask_done = P.sV.n

    for sl in range(NT):
        sy.wait_ge(P.sV.h, P.sV.n)
        sy.wait_ge(P.sS.h, P.sS.n)
        for k in range(KC):
            ins = sy.dma_start(out=sc["xslab"][k][:, :],
                               in_=xdram[k * 128:(k + 1) * 128, sl * 512:(sl + 1) * 512])
            P.sX.inc(ins, 16)
        for e in ("scalar", "tensor", "vector"):
            P.eng[e].wait_ge(P.sX.h, P.sX.n)
        ln_v = [xx[:, sl * 512:(sl + 1) * 512] for xx in sc["lnf"]]
        emit_ln(P, [xx[:, :] for xx in sc["xslab"]], ln_v, 512, sc)
    ln_done = P.sV.n

    npair = 8
    wq_free = [0, 0]
    exp_scale = 8.0 if l3 else 0.125
    ps_free = [0, 0]
    ps1_free = [0, 0]
    pt_free = [0, 0]
    pts_free = [0, 0]
    ppt_free = 0
    cols_free = 0
    ob_done_s = 0
    ob_done_v = 0
    for hp in range(npair):
        wpar = hp % 2
        sw = P.sWp[wpar]
        g.wait_ge(P.sT.h, wq_free[wpar])
        if l3:
            srcs = [wdram["l3a_wq"][:, hp * 128:(hp + 1) * 128],
                    wdram["l3a_wk"][:, hp * 128:(hp + 1) * 128],
                    wdram["l3a_wv"][:, hp * 128:(hp + 1) * 128]]
        else:
            wq = wdram["l0a_wqkv"]
            srcs = [wq[:, hp * 128:(hp + 1) * 128],
                    wq[:, 1024 + hp * 128:1024 + (hp + 1) * 128],
                    wq[:, 2048 + hp * 128:2048 + (hp + 1) * 128]]
        for si, src in enumerate(srcs):
            ins = g.dma_start(
                out=sc["wqkv"][wpar][:, :].rearrange("p (k m) -> p k m", k=KC)[:, :, si * 128:(si + 1) * 128],
                in_=src.rearrange("(k p) m -> p k m", p=128))
            sw.inc(ins, 16)
        if not l3:
            for si in range(3):
                ins = g.dma_start(out=sc["bias"][0:1, wpar * 384 + si * 128:wpar * 384 + (si + 1) * 128],
                                  in_=wdram["l0a_bias"][0:1, si * 1024 + hp * 128:si * 1024 + (hp + 1) * 128])
                sw.inc(ins, 16)
        w_ready = sw.n

        t.wait_ge(sw.h, w_ready)
        t.wait_ge(P.sV.h, max(ln_done, mask_done))
        for io, dstname in ((0, "qT"), (1, "kT")):
            for nb in range(NT):
                par = nb % 2
                t.wait_ge(P.sS.h, P.sS.n)
                t.wait_ge(P.sV.h, P.sV.n)
                for k in range(KC):
                    ins = t.matmul(sc["pQK"][par][:, 0:512],
                                   sc["wqkv"][wpar][:, k * 384 + io * 128:k * 384 + io * 128 + 128],
                                   sc["lnf"][k][:, nb * 512:(nb + 1) * 512],
                                   start=(k == 0), stop=(k == KC - 1 and l3))
                if not l3:
                    ins = t.matmul(sc["pQK"][par][:, 0:512],
                                   sc["bias"][0:1, wpar * 384 + io * 128:wpar * 384 + io * 128 + 128],
                                   P.ones_rowb[0:1, 0:512], start=False, stop=True)
                P.sT.inc(ins)
                if l3:
                    s.wait_ge(P.sT.h, P.sT.n)
                    ins = s.copy(out=sc["sqn"][:, 0:512], in_=sc["pQK"][par][:, 0:512])
                    P.sS.inc(ins)
                    s.wait_ge(P.sS.h, P.sS.n)
                    ins = s.square(out=sc["sqn"][:, 0:512], in_=sc["sqn"][:, 0:512])
                    P.sS.inc(ins)
                    for side in range(2):
                        b0 = 64 * side
                        t.wait_ge(P.sS.h, P.sS.n)
                        t.wait_ge(P.sV.h, P.sV.n)
                        ins = t.matmul(sc["pR"][0:1, 0:512], sc["onecol"][b0:b0 + 64, 0:1],
                                       sc["sqn"][b0:b0 + 64, 0:512], start=True, stop=True)
                        P.sT.inc(ins)
                        s.wait_ge(P.sT.h, P.sT.n)
                        row = sc["rows"][:, 0:512]
                        ins = s.activation(out=row, in_=sc["pR"][0:1, 0:512], func=AF.Identity)
                        P.sS.inc(ins)
                        v.wait_ge(P.sS.h, P.sS.n)
                        ins = v.tensor_scalar(out=row, in0=row, scalar1=1e-24, scalar2=None, op0=OP.max)
                        P.sV.inc(ins)
                        s.wait_ge(P.sV.h, P.sV.n)
                        s.activation(out=row, in_=row, func=AF.Ln)
                        s.wait_ge(P.sS.h, P.sS.n)
                        ins = s.activation(out=row, in_=row, func=AF.Exp, scale=-0.5)
                        P.sS.inc(ins)
                        t.wait_ge(P.sS.h, P.sS.n)
                        ins = t.matmul(sc["pR"][b0:b0 + 64, 0:512],
                                       P.ones_row[0:1, 0:64], row, start=True, stop=True,
                                       tile_position=(0, b0) if b0 else None)
                        P.sT.inc(ins)
                        s.wait_ge(P.sT.h, P.sT.n)
                        ins = s.copy(out=sc["sqn"][b0:b0 + 64, 0:512], in_=sc["pR"][b0:b0 + 64, 0:512])
                        P.sS.inc(ins)
                        v.wait_ge(P.sS.h, P.sS.n)
                        ins = v.tensor_tensor(out=sc[dstname][b0:b0 + 64, nb * 512:(nb + 1) * 512],
                                              in0=sc["pQK"][par][b0:b0 + 64, 0:512],
                                              in1=sc["sqn"][b0:b0 + 64, 0:512], op=OP.mult)
                        P.sV.inc(ins)
                else:
                    s.wait_ge(P.sT.h, P.sT.n)
                    ins = s.copy(out=sc[dstname][:, nb * 512:(nb + 1) * 512], in_=sc["pQK"][par][:, 0:512])
                    P.sS.inc(ins)
        for ti in range(T // 128):
            par = ti % 2
            t.wait_ge(P.sS.h, P.sS.n)
            t.wait_ge(P.sV.h, P.sV.n)
            for k in range(KC):
                ins = t.matmul(sc["pQK"][par][:, 0:128],
                               sc["lnf"][k][:, ti * 128:(ti + 1) * 128],
                               sc["wqkv"][wpar][:, k * 384 + 256:k * 384 + 384],
                               start=(k == 0), stop=(k == KC - 1 and l3))
            if not l3:
                ins = t.matmul(sc["pQK"][par][:, 0:128], P.ones_rowb[0:1, 0:128],
                               sc["bias"][0:1, wpar * 384 + 256:wpar * 384 + 384],
                               start=False, stop=True)
            P.sT.inc(ins)
            s.wait_ge(P.sT.h, P.sT.n)
            ins = s.copy(out=sc["vtm"][:, ti * 128:(ti + 1) * 128], in_=sc["pQK"][par][:, 0:128])
            P.sS.inc(ins)
        wq_free[wpar] = P.sT.n

        CA = sc["mbB"]
        t.wait_ge(P.sS.h, P.sS.n)
        t.wait_ge(P.sV.h, P.sV.n)
        for side in range(2):
            b0 = 64 * side
            for qi in range(T // 128):
                nkb = (qi * 128 + 128 + 511) // 512
                a = qi % 4
                qv = sc["qT"][b0:b0 + 64, qi * 128:(qi + 1) * 128]
                if not l3:
                    for kb in range(nkb):
                        par = kb % 2
                        t.wait_ge(P.sV.h, ps1_free[par])
                        t.wait_ge(P.sS.h, ps_free[par])
                        ins = t.matmul(sc["pS"][par][:, 0:512], qv,
                                       sc["kT"][b0:b0 + 64, kb * 512:(kb + 1) * 512],
                                       start=True, stop=True)
                        P.sT.inc(ins)
                        v.wait_ge(P.sT.h, P.sT.n)
                        if kb == nkb - 1:
                            ins = v.tensor_tensor(out=sc["pS"][par][:, 0:512],
                                                  in0=sc["pS"][par][:, 0:512],
                                                  in1=sc["maskv"][a],
                                                  op=OP.add)
                            P.sV.inc(ins)
                        ins = v.tensor_reduce(out=CA[:, 80 + kb:81 + kb],
                                              in_=sc["pS"][par][:, 0:512],
                                              axis=AX.X, op=OP.max, negate=True)
                        P.sV.inc(ins)
                        ps1_free[par] = P.sV.n
                    v.tensor_reduce(out=CA[:, 66:67], in_=CA[:, 80:80 + nkb],
                                    axis=AX.X, op=OP.min)
                    ins = v.tensor_scalar(out=CA[:, 66:67], in0=CA[:, 66:67],
                                          scalar1=float(exp_scale), scalar2=None, op0=OP.mult)
                    P.sV.inc(ins)
                    bias_ready = P.sV.n
                    bias_col = CA[:, 66:67]
                else:
                    bias_ready = 0
                    bias_col = P.const(-8.0, wide=True)
                mask_ns = [0] * nkb
                mm_ns = [0] * nkb

                def p2_scores(kb):
                    par = kb % 2
                    t.wait_ge(P.sS.h, ps_free[par])
                    if not l3:
                        t.wait_ge(P.sV.h, ps1_free[par])
                    ins = t.matmul(sc["pS"][par][:, 0:512], qv,
                                   sc["kT"][b0:b0 + 64, kb * 512:(kb + 1) * 512],
                                   start=True, stop=True)
                    P.sT.inc(ins)
                    mm_ns[kb] = P.sT.n
                    if kb == nkb - 1:
                        v.wait_ge(P.sT.h, P.sT.n)
                        ins = v.tensor_tensor(out=sc["pS"][par][:, 0:512],
                                              in0=sc["pS"][par][:, 0:512],
                                              in1=sc["maskv"][a], op=OP.add)
                        P.sV.inc(ins)
                        mask_ns[kb] = P.sV.n

                p2_scores(0)
                for kb in range(nkb):
                    par = kb % 2
                    if kb + 1 < nkb:
                        p2_scores(kb + 1)
                    s.wait_ge(P.sT.h, mm_ns[kb])
                    s.wait_ge(P.sV.h, max(bias_ready, mask_ns[kb], cols_free))
                    s.wait_ge(P.sT.h, pt_free[par])
                    ins = s.activation(out=sc["Pt"][par][:, 0:512], in_=sc["pS"][par][:, 0:512],
                                       func=AF.Exp, scale=P.const(exp_scale, wide=True),
                                       bias=bias_col, accum_out=CA[:, 72 + kb:73 + kb])
                    P.sS.inc(ins)
                    ps_free[par] = P.sS.n
                    exp_n = P.sS.n
                    t.wait_ge(P.sS.h, exp_n)
                    for c in range(4):
                        ins = t.transpose(sc["pPT"][:, c * 128:(c + 1) * 128],
                                          sc["Pt"][par][:, c * 128:(c + 1) * 128],
                                          sc["identbb"][:])
                    P.sT.inc(ins)
                    pt_free[par] = P.sT.n
                    s.wait_ge(P.sT.h, P.sT.n)
                    ins = s.copy(out=sc["PTs"][par][:, 0:512], in_=sc["pPT"][:, 0:512])
                    P.sS.inc(ins)
                    ppt_free = P.sS.n
                    t.wait_ge(P.sS.h, ppt_free)
                    if kb == 0:
                        t.wait_ge(P.sS.h, ob_done_s)
                        t.wait_ge(P.sV.h, ob_done_v)
                    for c in range(4):
                        ins = t.matmul(sc["pO"][:, 0:64], sc["PTs"][par][:, c * 128:(c + 1) * 128],
                                       sc["vtm"][:, (kb * 4 + c) * 128 + b0:(kb * 4 + c) * 128 + b0 + 64],
                                       start=(kb == 0 and c == 0), stop=(kb == nkb - 1 and c == 3))
                    P.sT.inc(ins)
                    pts_free[par] = P.sT.n
                po_done = P.sT.n
                v.wait_ge(P.sS.h, exp_n)
                v.tensor_reduce(out=CA[:, 64:65], in_=CA[:, 72:72 + nkb], axis=AX.X, op=OP.add)
                ins = v.reciprocal(out=CA[:, 65:66], in_=CA[:, 64:65])
                P.sV.inc(ins)
                cols_free = P.sV.n
                v.wait_ge(P.sT.h, po_done)
                ins = v.tensor_scalar(out=sc["Ob"][:, :], in0=sc["pO"][:, 0:64],
                                      scalar1=CA[:, 65:66], scalar2=None, op0=OP.mult)
                P.sV.inc(ins)
                ob_done_v = P.sV.n
                t.wait_ge(P.sV.h, ob_done_v)
                t.wait_ge(P.sS.h, ppt_free)
                ins = t.transpose(sc["pPT"][0:64, 0:128], sc["Ob"][:, 0:64], sc["identbb"][:])
                P.sT.inc(ins)
                s.wait_ge(P.sT.h, P.sT.n)
                ins = s.copy(out=sc["otF"][hp][b0:b0 + 64, qi * 128:(qi + 1) * 128],
                             in_=sc["pPT"][0:64, 0:128])
                P.sS.inc(ins)
                ob_done_s = P.sS.n

    wsrc = wdram["l3a_wo"] if l3 else wdram["l0a_wproj"]
    otF_done = P.sS.n
    t.wait_ge(P.sS.h, otF_done)
    P.xp_cons = [max(P.xp_cons[0], ln_done), max(P.xp_cons[1], ln_done)]
    for fh in range(4):
        g.wait_ge(P.sT.h, P.sT.n)
        ins = g.dma_start(out=sc["wpro"][:, :].rearrange("p (k m) -> p k m", k=KC),
                          in_=wsrc[:, fh * 256:(fh + 1) * 256].rearrange("(k p) m -> p k m", p=128))
        P.sB.inc(ins, 16)
        if not l3 and fh == 0:
            ins = g.dma_start(out=sc["bias"][0:1, 0:1024], in_=wdram["l0a_bias"][0:1, 3072:4096])
            P.sB.inc(ins, 16)
        wpro_ready = P.sB.n
        t.wait_ge(P.sB.h, wpro_ready)
        _emit_outproj_half(P, sc, xdram, NT, l3, fh)
    P.bias_gate = P.sT.n


def _emit_outproj_half(P, sc, xdram, NT, l3, fh):
    s, t, v, g, sy = (P.eng[e] for e in ("scalar", "tensor", "vector", "gpsimd", "sync"))
    for f in range(fh * 2, fh * 2 + 2):
        for nb in range(NT):
            par = (f * NT + nb) % 2
            sxp = P.sXp[par]
            sy.wait_ge(P.sV.h, P.xp_cons[par])
            ins = sy.dma_start(out=sc["xbuf"][par][:, :],
                               in_=xdram[f * 128:(f + 1) * 128, nb * 512:(nb + 1) * 512])
            sxp.inc(ins, 16)
            t.wait_ge(P.sV.h, P.att_pq_free[par])
            fl = f - fh * 2
            for k in range(KC):
                ins = t.matmul(sc["pQK"][par][:, 0:512],
                               sc["wpro"][:, k * 256 + fl * 128:k * 256 + fl * 128 + 128],
                               sc["otF"][k][:, nb * 512:(nb + 1) * 512],
                               start=(k == 0), stop=(k == KC - 1 and l3))
            if not l3:
                ins = t.matmul(sc["pQK"][par][:, 0:512],
                               sc["bias"][0:1, f * 128:f * 128 + 128],
                               P.ones_rowb[0:1, 0:512], start=False, stop=True)
            P.sT.inc(ins)
            v.wait_ge(P.sT.h, P.sT.n)
            v.wait_ge(sxp.h, sxp.n)
            v.wait_ge(P.sO.h, P.sO.n)
            ins = v.tensor_tensor(out=sc["xnew"][par][:, :], in0=sc["pQK"][par][:, 0:512],
                                  in1=sc["xbuf"][par][:, :], op=OP.add)
            P.sV.inc(ins)
            P.xp_cons[par] = P.sV.n
            P.att_pq_free[par] = P.sV.n
            sy.wait_ge(P.sV.h, P.sV.n)
            ins = sy.dma_start(out=xdram[f * 128:(f + 1) * 128, nb * 512:(nb + 1) * 512],
                               in_=sc["xnew"][par][:, :])
            P.sO.inc(ins, 16)
            P.xn_store[par] = P.sO.n




def alloc_l2a_scratch(P, ctx, T):
    nc = P.nc
    sc = {}
    sc["onecol"] = ctx.enter_context(nc.sbuf_tensor("lonec", [128, 1], F32))
    sc["onecolb"] = ctx.enter_context(nc.sbuf_tensor("lonecb", [128, 1], BF16))
    sc["xslab"] = [ctx.enter_context(nc.sbuf_tensor(f"lxs{k}", [128, 512], F32)) for k in range(KC)]
    sc["lnt"] = [ctx.enter_context(nc.sbuf_tensor(f"llnt{k}", [128, 512], BF16)) for k in range(KC)]
    sc["sq"] = [ctx.enter_context(nc.sbuf_tensor(f"lsq{k}", [128, 512], BF16)) for k in range(2)]
    sc["sq"] = sc["sq"] * 4
    sc["rows"] = ctx.enter_context(nc.sbuf_tensor("lrows", [1, 3 * 512], F32))
    sc["rbB"] = ctx.enter_context(nc.sbuf_tensor("lrbB", [128, 512], F32))
    sc["mbB"] = ctx.enter_context(nc.sbuf_tensor("lmbB", [128, 512], F32))
    sc["tmp"] = ctx.enter_context(nc.sbuf_tensor("ltmp", [128, 512], F32))
    sc["identbb"] = ctx.enter_context(nc.sbuf_tensor("lidbb", [128, 128], BF16))
    sc["Xtm"] = ctx.enter_context(nc.sbuf_tensor("lXtm", [128, 1024], BF16))
    sc["Xtm32"] = ctx.enter_context(nc.sbuf_tensor("lXtm32", [128, 1024], F32))
    sc["Xfm"] = ctx.enter_context(nc.sbuf_tensor("lXfm", [128, 1024], BF16))
    sc["maskt"] = ctx.enter_context(nc.sbuf_tensor("lmask", [128, 128], F32))
    sc["Pb"] = ctx.enter_context(nc.sbuf_tensor("lPb", [128, 128], BF16))
    sc["bsel"] = ctx.enter_context(nc.sbuf_tensor("lbsel", [128, 128], BF16))
    sc["PTs"] = ctx.enter_context(nc.sbuf_tensor("lPTs", [128, 128], BF16))
    sc["cols"] = ctx.enter_context(nc.sbuf_tensor("lcols", [128, 4], F32))
    sc["ot"] = ctx.enter_context(nc.sbuf_tensor("lot", [128, 1024], F32))
    sc["att"] = [ctx.enter_context(nc.sbuf_tensor(f"latt{i}", [128, 1024], F32)) for i in range(2)]
    sc["identb"] = ctx.enter_context(nc.sbuf_tensor("lidb", [128, 128], F32))
    sc["xb"] = [ctx.enter_context(nc.sbuf_tensor(f"lxb{i}", [128, 512], F32)) for i in range(2)]
    sc["pS"] = ctx.enter_context(nc.psum_tensor("lpS", [128, 128], F32))
    sc["pTf"] = ctx.enter_context(nc.psum_tensor("lpTf", [128, 128], F32))
    sc["pT"] = ctx.enter_context(nc.psum_tensor("lpT", [128, 1024], BF16))
    sc["pO"] = ctx.enter_context(nc.psum_tensor("lpO", [128, 512], F32))
    sc["pM"] = ctx.enter_context(nc.psum_tensor("lpM", [1, 512], F32))
    sc["pQ"] = ctx.enter_context(nc.psum_tensor("lpQ", [1, 512], F32))
    sc["pB"] = ctx.enter_context(nc.psum_tensor("lpB", [128, 512], F32))
    sc["pB2"] = ctx.enter_context(nc.psum_tensor("lpB2", [128, 512], F32))
    return sc


def emit_l2attn_block(P, xdram, T, sc, contrib, gath, gath2, myout, mask_in, bsel_in, n_cores):
    s, t, v, g, sy = (P.eng[e] for e in ("scalar", "tensor", "vector", "gpsimd", "sync"))
    NT = T // 512
    NB = n_cores
    PS = 128 // NB

    ins = sy.dma_start(out=sc["maskt"][:, :], in_=mask_in)
    P.sX.inc(ins, 16)
    ins = g.dma_start(out=sc["bsel"][:, :], in_=bsel_in)
    P.sB.inc(ins, 16)

    for sl in range(NT):
        sy.wait_ge(P.sV.h, P.sV.n)
        sy.wait_ge(P.sS.h, P.sS.n)
        for k in range(KC):
            ins = sy.dma_start(out=sc["xslab"][k][:, :],
                               in_=xdram[k * 128:(k + 1) * 128, sl * 512:(sl + 1) * 512])
            P.sX.inc(ins, 16)
        for e in ("scalar", "tensor", "vector"):
            P.eng[e].wait_ge(P.sX.h, P.sX.n)
        emit_ln(P, [xx[:, :] for xx in sc["xslab"]], [xx[:, :] for xx in sc["lnt"]], 512, sc)
        for tt in range(4):
            t.wait_ge(P.sV.h, P.sV.n)
            t.wait_ge(P.sS.h, P.sS.n)
            for k in range(KC):
                ins = t.transpose(sc["pT"][:, k * 128:(k + 1) * 128],
                                  sc["lnt"][k][:, tt * 128:(tt + 1) * 128], sc["identbb"][:])
            P.sT.inc(ins)
            s.wait_ge(P.sT.h, P.sT.n)
            s.wait_ge(P.sO.h, P.sO.n)
            ins = s.copy(out=sc["Xtm32"][:, :], in_=sc["pT"][:, 0:1024])
            P.sS.inc(ins)
            sy.wait_ge(P.sS.h, P.sS.n)
            ins = sy.dma_start(out=contrib[(sl * 4 + tt) * 128:(sl * 4 + tt + 1) * 128, :],
                               in_=sc["Xtm32"][:, :])
            P.sO.inc(ins, 16)

    g.wait_ge(P.sO.h, P.sO.n)
    ins = g.collective_compute("AllGather", OP.bypass,
                               replica_groups=[list(range(n_cores))],
                               ins=[contrib[:, :]], outs=[gath[:, :]])
    P.sG.inc(ins, 1)
    sy.wait_ge(P.sG.h, P.sG.n)
    g2v = gath2.rearrange("(gg r) d -> gg r d", r=128)
    for c in range(NB):
        src = gath[c * T:(c + 1) * T, :].rearrange("(gg ps) d -> gg ps d", ps=PS)
        ins = sy.dma_start(out=g2v[:, c * PS:(c + 1) * PS, :], in_=src)
        P.sX.inc(ins, 16)
    for e in ("tensor", "scalar", "vector", "gpsimd"):
        P.eng[e].wait_ge(P.sX.h, P.sX.n)
    t.wait_ge(P.sB.h, P.sB.n)
    ngroups = T // PS
    for grp in range(ngroups):
        g.wait_ge(P.sS.h, P.sS.n)
        g.wait_ge(P.sT.h, P.sT.n)
        ins = g.dma_start(out=sc["Xtm"][:, :],
                          in_=gath2[grp * 128:(grp + 1) * 128, :])
        P.sW.inc(ins, 16)
        t.wait_ge(P.sW.h, P.sW.n)
        for k in range(KC):
            t.wait_ge(P.sS.h, P.sS.n)
            ins = t.transpose(sc["pT"][:, 0:128], sc["Xtm"][:, k * 128:(k + 1) * 128],
                              sc["identbb"][:])
            P.sT.inc(ins)
            s.wait_ge(P.sT.h, P.sT.n)
            ins = s.copy(out=sc["Xfm"][:, k * 128:(k + 1) * 128], in_=sc["pT"][0:128, 0:128])
            P.sS.inc(ins)
        t.wait_ge(P.sS.h, P.sS.n)
        t.wait_ge(P.sV.h, P.sV.n)
        for k in range(KC):
            ins = t.matmul(sc["pS"][:, 0:128], sc["Xfm"][:, k * 128:(k + 1) * 128],
                           sc["Xfm"][:, k * 128:(k + 1) * 128], start=(k == 0), stop=(k == KC - 1))
        P.sT.inc(ins)
        v.wait_ge(P.sT.h, P.sT.n)
        v.wait_ge(P.sX.h, P.sX.n)
        v.tensor_tensor(out=sc["pS"][:, 0:128], in0=sc["pS"][:, 0:128],
                        in1=sc["maskt"][:, :], op=OP.add)
        v.tensor_reduce(out=sc["cols"][:, 0:1], in_=sc["pS"][:, 0:128], axis=AX.X,
                        op=OP.max, negate=True)
        ins = v.tensor_scalar(out=sc["cols"][:, 0:1], in0=sc["cols"][:, 0:1],
                              scalar1=1.0 / 32.0, scalar2=None, op0=OP.mult)
        P.sV.inc(ins)
        s.wait_ge(P.sV.h, P.sV.n)
        ins = s.activation(out=sc["Pb"][:, :], in_=sc["pS"][:, 0:128], func=AF.Exp,
                           scale=P.const(1.0 / 32.0, wide=True), bias=sc["cols"][:, 0:1],
                           accum_out=sc["cols"][:, 1:2])
        P.sS.inc(ins)
        t.wait_ge(P.sS.h, P.sS.n)
        ins = t.matmul(sc["pTf"][:, 0:128], sc["bsel"][:, :], sc["Pb"][:, :],
                       start=True, stop=True)
        P.sT.inc(ins)
        s.wait_ge(P.sT.h, P.sT.n)
        ins = s.copy(out=sc["PTs"][:, :], in_=sc["pTf"][0:128, 0:128])
        P.sS.inc(ins)
        v.wait_ge(P.sS.h, P.sS.n)
        v.tensor_reduce(out=sc["cols"][:, 1:2], in_=sc["PTs"][:, :], axis=AX.X, op=OP.add)
        ins = v.reciprocal(out=sc["cols"][:, 2:3], in_=sc["cols"][:, 1:2])
        P.sV.inc(ins)
        t.wait_ge(P.sV.h, P.sV.n)
        ins = t.transpose(sc["pT"][:, 0:128], sc["PTs"][:, :], sc["identbb"][:])
        P.sT.inc(ins)
        s.wait_ge(P.sT.h, P.sT.n)
        ins = s.copy(out=sc["PTs"][:, :], in_=sc["pT"][0:128, 0:128])
        P.sS.inc(ins)
        t.wait_ge(P.sS.h, P.sS.n)
        for nn in range(2):
            t.wait_ge(P.sV.h, P.sV.n)
            ins = t.matmul(sc["pO"][:, 0:512], sc["PTs"][:, :],
                           sc["Xtm"][:, nn * 512:(nn + 1) * 512], start=True, stop=True)
            P.sT.inc(ins)
            v.wait_ge(P.sT.h, P.sT.n)
            v.wait_ge(P.sO.h, P.sO.n)
            ins = v.tensor_scalar(out=sc["ot"][:, nn * 512:(nn + 1) * 512], in0=sc["pO"][:, 0:512],
                                  scalar1=sc["cols"][:, 2:3], scalar2=None, op0=OP.mult)
            P.sV.inc(ins)
        sy.wait_ge(P.sV.h, P.sV.n)
        ins = sy.dma_start(out=myout[grp * PS:(grp + 1) * PS, :], in_=sc["ot"][0:PS, :])
        P.sO.inc(ins, 16)

    sy.wait_ge(P.sO.h, P.sO.n)
    for ti in range(T // 128):
        par = ti % 2
        g.wait_ge(P.sV.h, P.l2a_att_cons[par])
        ins = g.dma_start(out=sc["att"][par][:, :], in_=myout[ti * 128:(ti + 1) * 128, :])
        P.sWp[par].inc(ins, 16)
        t.wait_ge(P.sWp[par].h, P.sWp[par].n)
        for k in range(KC):
            t.wait_ge(P.sV.h, P.sV.n)
            ins = t.transpose(sc["pTf"][:, 0:128], sc["att"][par][:, k * 128:(k + 1) * 128],
                              sc["identb"][:])
            P.sT.inc(ins)
            xp = (ti * KC + k) % 2
            sy.wait_ge(P.sV.h, P.xp_cons[xp])
            ins2 = sy.dma_start(out=sc["xb"][xp][:, 0:128],
                                in_=xdram[k * 128:(k + 1) * 128, ti * 128:(ti + 1) * 128])
            P.sXp[xp].inc(ins2, 16)
            v.wait_ge(P.sT.h, P.sT.n)
            v.wait_ge(P.sXp[xp].h, P.sXp[xp].n)
            v.wait_ge(P.sO.h, P.sO.n)
            ins = v.tensor_tensor(out=sc["xb"][xp][:, 0:128], in0=sc["pTf"][0:128, 0:128],
                                  in1=sc["xb"][xp][:, 0:128], op=OP.add)
            P.sV.inc(ins)
            P.xp_cons[xp] = P.sV.n
            P.l2a_att_cons[par] = P.sV.n
            sy.wait_ge(P.sV.h, P.sV.n)
            ins = sy.dma_start(out=xdram[k * 128:(k + 1) * 128, ti * 128:(ti + 1) * 128],
                               in_=sc["xb"][xp][:, 0:128])
            P.sO.inc(ins, 16)




def alloc_l1a_scratch(P, ctx, T):
    nc = P.nc
    sc = {}
    sc["onecol"] = ctx.enter_context(nc.sbuf_tensor("m1c", [128, 1], F32))
    sc["onecolb"] = ctx.enter_context(nc.sbuf_tensor("m1cb", [128, 1], BF16))
    sc["xln"] = [ctx.enter_context(nc.sbuf_tensor(f"mxln{k}", [128, T], BF16)) for k in range(KC)]
    sc["yslab"] = [ctx.enter_context(nc.sbuf_tensor(f"mysl{k}", [128, 512], BF16)) for k in range(KC)]
    sc["lnh"] = [ctx.enter_context(nc.sbuf_tensor(f"mlnh{k}", [128, 512], BF16)) for k in range(KC)]
    sc["sq"] = [ctx.enter_context(nc.sbuf_tensor(f"msq{k}", [128, 512], BF16)) for k in range(2)]
    sc["sq"] = sc["sq"] * 4
    sc["rows"] = ctx.enter_context(nc.sbuf_tensor("mrows", [1, 3 * 512], F32))
    sc["drow"] = sc["rows"][0:1, 0:128]
    sc["rbB"] = ctx.enter_context(nc.sbuf_tensor("mrbB", [128, 512], F32))
    sc["bxs"] = sc["rbB"][0:16, :]
    sc["mbB"] = ctx.enter_context(nc.sbuf_tensor("mmbB", [128, 512], F32))
    sc["tmp"] = ctx.enter_context(nc.sbuf_tensor("mtmp", [128, 512], F32))
    sc["identbb"] = ctx.enter_context(nc.sbuf_tensor("midbb", [128, 128], BF16))
    sc["cols"] = ctx.enter_context(nc.sbuf_tensor("mcols", [128, 84], F32))
    w0 = ctx.enter_context(nc.sbuf_tensor("mwp0", [128, 8 * 384], BF16))
    sc["wqkv"] = [w0, w0]
    sc["kT"] = ctx.enter_context(nc.sbuf_tensor("mkT", [128, T], BF16))
    sc["rT"] = ctx.enter_context(nc.sbuf_tensor("mrT", [128, T], BF16))
    sc["vtm"] = ctx.enter_context(nc.sbuf_tensor("mvtm", [128, T], BF16))
    sc["xslab"] = [ctx.enter_context(nc.sbuf_tensor(f"mxs{k}", [128, 512], F32)) for k in range(KC)]
    sc["ctraj"] = ctx.enter_context(nc.sbuf_tensor("mctraj", [16, T], F32))
    _off_ctraj = nc.bytes_allocated(MemorySpace.SBUF) - T * 4
    sc["rtraj"] = ctx.enter_context(nc.sbuf_tensor("mrtraj", [16, T], F32))
    sc["F"] = ctx.enter_context(nc.sbuf_tensor("mF", [16, T], F32))
    _off_F = nc.bytes_allocated(MemorySpace.SBUF) - T * 4
    sc["dtb16"] = ctx.enter_context(nc.sbuf_tensor("mdtb16", [16, T], F32))
    _off_dtb16 = nc.bytes_allocated(MemorySpace.SBUF) - T * 4

    sc["states"] = sc["ctraj"].bitcast(BF16)[:, 0:T]
    sc["m16"] = ctx.enter_context(nc.sbuf_tensor("mm16", [16, 8 * 16], F32))
    sc["eep"] = ctx.enter_context(nc.sbuf_tensor("meep", [16, 64], F32))
    sc["svec"] = ctx.enter_context(nc.sbuf_tensor("msvec", [16, 8], F32))
    sc["srow"] = ctx.enter_context(nc.sbuf_tensor("msrow", [1, 16], F32))
    sc["sl1"] = ctx.enter_context(nc.sbuf_tensor("msl1", [16, 1], F32))
    sc["st1"] = ctx.enter_context(nc.sbuf_tensor("mst1", [16, 1], F32))
    sc["srdt"] = ctx.enter_context(nc.sbuf_tensor("msrdt", [16, 1], F32))
    sc["PWrevT"] = nc.alloc_sbuf_tensor_at("mPWrevT", [128, 64], BF16, offset=_off_dtb16)
    sc["PW"] = nc.alloc_sbuf_tensor_at("mPW", [64, 132], F32, offset=_off_dtb16 + 128)
    sc["rev127"] = sc["rows"][0:1, 128:256]
    _ovl = [("xb0", [128, 512], F32, 0), ("xb1", [128, 512], F32, 2048),
            ("miotts", [128, 128], F32, 4096), ("mca01", [128, 128], F32, 4608),
            ("mzer", [64, 132], F32, 5120), ("mWst", [64, 64], F32, 5664),
            ("mAmt", [128, 128], BF16, 5920), ("mrpb", [64, 128], BF16, 6176),
            ("mWb", [64, 64], BF16, 6432), ("mktm", [128, 64], BF16, 6560),
            ("mkhat", [128, 64], BF16, 6688), ("mrd64", [64, 1], F32, 6816),
            ("mlndB", [128, 1], F32, 6848)]
    if T * 4 >= 6880:
        _ov = {nm: nc.alloc_sbuf_tensor_at(nm, shp, dt, offset=_off_F + off)
               for (nm, shp, dt, off) in _ovl}
    else:
        _ov = {nm: ctx.enter_context(nc.sbuf_tensor(nm, shp, dt))
               for (nm, shp, dt, off) in _ovl}
    sc["xb"] = [_ov["xb0"], _ov["xb1"]]
    sc["iotts"], sc["caus01"], sc["zer"] = _ov["miotts"], _ov["mca01"], _ov["mzer"]
    sc["Wst"], sc["Amt"], sc["rpb"], sc["Wb"] = _ov["mWst"], _ov["mAmt"], _ov["mrpb"], _ov["mWb"]
    sc["ktm"], sc["khat"], sc["rd64"], sc["lndB"] = _ov["mktm"], _ov["mkhat"], _ov["mrd64"], _ov["mlndB"]
    sc["maskh"] = sc["xb"][0][:, 128:256]
    sc["m1t"] = sc["xb"][0][:, 0:128]
    sc["CwTb"] = sc["rtraj"].bitcast(BF16)[:, 0:1024]
    sc["comb"] = None
    n_ov = min(KC, (2 * T) // 1024)
    sc["xsl"] = [nc.alloc_sbuf_tensor_at(f"mxsl{k}", [128, 512], BF16,
                                         offset=_off_ctraj + 2 * T + k * 1024)
                 for k in range(n_ov)]
    sc["xsl"] += [ctx.enter_context(nc.sbuf_tensor(f"mxsl{k}", [128, 512], BF16))
                  for k in range(n_ov, KC)]
    sc["addl"] = None
    if T >= 4096:
        sc["wo0"], sc["wo1"] = sc["kT"], sc["rT"]
    else:
        sc["wo0"] = ctx.enter_context(nc.sbuf_tensor("mwo0", [128, 4096], BF16))
        sc["wo1"] = ctx.enter_context(nc.sbuf_tensor("mwo1", [128, 4096], BF16))
    sc["pA"] = ctx.enter_context(nc.psum_tensor("mpA", [128, 512], F32))
    sc["pB1"] = ctx.enter_context(nc.psum_tensor("mpB1", [128, 512], F32))
    sc["pY"] = ctx.enter_context(nc.psum_tensor("mpY", [64, 128], F32))
    sc["pU"] = ctx.enter_context(nc.psum_tensor("mpU", [64, 64], F32))
    sc["pKT"] = ctx.enter_context(nc.psum_tensor("mpKT", [128, 64], BF16))
    sc["pSc"] = ctx.enter_context(nc.psum_tensor("mpSc", [64, 4], F32))
    sc["pM"] = ctx.enter_context(nc.psum_tensor("mpM", [1, 512], F32))
    sc["pQst"] = ctx.enter_context(nc.psum_tensor("mpQst", [1, 512], F32))
    sc["pQ"] = sc["pQst"]
    sc["pB"] = sc["pA"]
    sc["pB2"] = sc["pB1"]
    return sc




def emit_l1attn_block(P, xdram, wdram, T, sc):
    s, t, v, g, sy = (P.eng[e] for e in ("scalar", "tensor", "vector", "gpsimd", "sync"))
    NT = T // 512
    C = sc["cols"]

    def col(i, rows=128):
        return C[0:rows, i:i + 1]

    ins = g.dma_start(out=C[:, :], in_=wdram["l1a_cols"][:, :])
    P.sB.inc(ins, 16)
    ins = g.dma_start(out=sc["wqkv"][0][:, 0:128].rearrange("p (k e) -> p k e", k=KC),
                      in_=wdram["l1a_BwT"].rearrange("(k p) e -> p k e", p=128))
    P.sB.inc(ins, 16)
    ins = g.dma_start(out=sc["wqkv"][0][:, 128:136].rearrange("p (k e) -> p k e", k=KC),
                      in_=wdram["l1a_dtwT"].rearrange("(k p) e -> p k e", p=128),
                      allow_slow_non_contiguous=True)
    P.sB.inc(ins, 16)
    ins = g.dma_start(out=sc["m16"][:, 0:32].rearrange("p (a e) -> p a e", a=2),
                      in_=wdram["l1a_AAT"].rearrange("(a p) e -> p a e", p=16))
    P.sB.inc(ins, 16)
    for e in ("scalar", "tensor", "vector", "gpsimd"):
        P.eng[e].wait_ge(P.sB.h, P.sB.n)

    for sl in range(NT):
        sy.wait_ge(P.sV.h, P.sV.n)
        sy.wait_ge(P.sS.h, P.sS.n)
        for k in range(KC):
            ins = sy.dma_start(out=sc["xslab"][k][:, :],
                               in_=xdram[k * 128:(k + 1) * 128, sl * 512:(sl + 1) * 512])
            P.sX.inc(ins, 16)
        for e in ("scalar", "tensor", "vector"):
            P.eng[e].wait_ge(P.sX.h, P.sX.n)
        emit_ln(P, [xx[:, :] for xx in sc["xslab"]],
                [xx[:, sl * 512:(sl + 1) * 512] for xx in sc["xln"]], 512, sc)
    xln_done = P.sV.n

    I16 = sc["m16"][:, 80:96]
    C16 = sc["m16"][:, 96:112]
    ids = sc["m16"][:, 112:128]
    ins = g.iota(ids, pattern=[[1, 16]], base=0, channel_multiplier=-1,
                 allow_small_or_imprecise_dtypes=True)
    P.sG.inc(ins)
    v.wait_ge(P.sG.h, P.sG.n)
    v.tensor_scalar(out=I16, in0=ids, scalar1=0.0, scalar2=None, op0=OP.is_equal)
    ins = v.tensor_scalar(out=C16, in0=I16, scalar1=-0.0625, scalar2=None, op0=OP.add)
    P.sV.inc(ins)
    for e in ("tensor", "scalar"):
        P.eng[e].wait_ge(P.sV.h, P.sV.n)

    for sl in range(NT):
        xv = [sc["xln"][k][:, sl * 512:(sl + 1) * 512] for k in range(KC)]
        t.wait_ge(P.sV.h, P.sV.n)
        t.wait_ge(P.sS.h, P.sS.n)
        emit_ln(P, xv, [xx[:, :] for xx in sc["xsl"]], 512, sc,
                affine=([col(0 + k) for k in range(KC)], [col(8 + k) for k in range(KC)]))
        t.wait_ge(P.sV.h, P.sV.n)
        for k in range(KC):
            ins = t.matmul(sc["pM"][0:1, 0:512], sc["wqkv"][0][:, 128 + k:129 + k],
                           sc["xsl"][k][:, :], start=(k == 0), stop=(k == KC - 1))
        P.sT.inc(ins)
        for k in range(KC):
            ins = t.matmul(sc["pB1"][0:16, 0:512],
                           sc["wqkv"][0][:, k * 16:(k + 1) * 16],
                           sc["xsl"][k][:, :], start=(k == 0), stop=(k == KC - 1))
        P.sT.inc(ins)
        s.wait_ge(P.sT.h, P.sT.n)
        ins = s.copy(out=sc["bxs"][:, 0:512], in_=sc["pB1"][0:16, 0:512])
        P.sS.inc(ins)
        t.wait_ge(P.sS.h, P.sS.n)
        ins = t.matmul(sc["pB1"][0:16, 0:512], C16, sc["bxs"][:, 0:512], start=True, stop=True)
        P.sT.inc(ins)
        s.wait_ge(P.sT.h, P.sT.n)
        ins = s.copy(out=sc["F"][:, sl * 512:(sl + 1) * 512], in_=sc["pB1"][0:16, 0:512])
        P.sS.inc(ins)
        R0 = sc["rows"][0:1, 0:512]
        R1 = sc["rows"][0:1, 512:1024]
        R2 = sc["rows"][0:1, 1024:1536]
        v.wait_ge(P.sT.h, P.sT.n)
        ins = v.tensor_scalar(out=R2, in0=sc["pM"][0:1, 0:512], scalar1=C[0:1, 66:67],
                              scalar2=None, op0=OP.add)
        P.sV.inc(ins)
        v.tensor_scalar(out=R0, in0=R2, scalar1=-1.0, scalar2=None, op0=OP.mult)
        v.tensor_tensor(out=R0, in0=R0, in1=R2, op=OP.max)
        ins = v.tensor_scalar(out=R1, in0=R2, scalar1=0.0, scalar2=None, op0=OP.max)
        P.sV.inc(ins)
        s.wait_ge(P.sV.h, P.sV.n)
        ins = s.activation(out=R0, in_=R0, func=AF.Exp, scale=-1.0)
        P.sS.inc(ins)
        s.wait_ge(P.sS.h, P.sS.n)
        ins = s.activation(out=R0, in_=R0, func=AF.Ln, bias=1.0)
        P.sS.inc(ins)
        v.wait_ge(P.sS.h, P.sS.n)
        v.tensor_tensor(out=R0, in0=R0, in1=R1, op=OP.add)
        ins = v.tensor_scalar(out=R2, in0=R0, scalar1=0.001, scalar2=0.1,
                              op0=OP.max, op1=OP.min)
        P.sV.inc(ins)
        t.wait_ge(P.sV.h, P.sV.n)
        ins = t.matmul(sc["pB1"][0:16, 0:512], P.ones_row[0:1, 0:16], R2, start=True, stop=True)
        P.sT.inc(ins)
        s.wait_ge(P.sT.h, P.sT.n)
        ins = s.copy(out=sc["dtb16"][:, sl * 512:(sl + 1) * 512], in_=sc["pB1"][0:16, 0:512])
        P.sS.inc(ins)
    v.wait_ge(P.sS.h, P.sS.n)
    ins = v.tensor_copy(out=sc["ctraj"][:, 0:1], in_=sc["F"][:, 0:1])
    P.sV.inc(ins)

    A_t = sc["m16"][:, 0:16]
    AT_t = sc["m16"][:, 16:32]
    B_t = sc["m16"][:, 32:48]
    t.wait_ge(P.sS.h, P.sS.n)
    t.wait_ge(P.sV.h, P.sV.n)
    ins = t.matmul(sc["pSc"][0:16, 0:1] if False else sc["pB1"][0:16, 0:16], A_t, A_t,
                   start=True, stop=True)
    P.sT.inc(ins)
    s.wait_ge(P.sT.h, P.sT.n)
    ins = s.copy(out=B_t, in_=sc["pB1"][0:16, 0:16])
    P.sS.inc(ins)
    ins = v.memset(sc["svec"][:, 0:1], 0.25)
    P.sV.inc(ins)
    NPI = 192
    for it in range(NPI):
        t.wait_ge(P.sV.h, P.sV.n)
        t.wait_ge(P.sS.h, P.sS.n)
        ins = t.matmul(sc["pSc"][0:16, 0:1], B_t, sc["svec"][:, 0:1], start=True, stop=True)
        P.sT.inc(ins)
        s.wait_ge(P.sT.h, P.sT.n)
        ins = s.copy(out=sc["svec"][:, 1:2], in_=sc["pSc"][0:16, 0:1])
        P.sS.inc(ins)
        t.wait_ge(P.sS.h, P.sS.n)
        ins = t.matmul(sc["pSc"][0:1, 1:2], sc["svec"][:, 1:2], sc["svec"][:, 1:2],
                       start=True, stop=True)
        P.sT.inc(ins)
        s.wait_ge(P.sT.h, P.sT.n)
        s.activation(out=sc["srow"][0:1, 0:1], in_=sc["pSc"][0:1, 1:2], func=AF.Ln)
        s.wait_ge(P.sS.h, P.sS.n)
        ins = s.activation(out=sc["srow"][0:1, 1:2], in_=sc["srow"][0:1, 0:1],
                           func=AF.Exp, scale=-0.5)
        P.sS.inc(ins)
        t.wait_ge(P.sS.h, P.sS.n)
        ins = t.matmul(sc["pSc"][0:16, 2:3], P.ones_row[0:1, 0:16],
                       sc["srow"][0:1, 1:2], start=True, stop=True)
        P.sT.inc(ins)
        v.wait_ge(P.sT.h, P.sT.n)
        ins = v.tensor_tensor(out=sc["svec"][:, 0:1], in0=sc["svec"][:, 1:2],
                              in1=sc["pSc"][0:16, 2:3], op=OP.mult)
        P.sV.inc(ins)
        if it == NPI - 1:
            ins = s.activation(out=sc["srow"][0:1, 2:3], in_=sc["srow"][0:1, 0:1],
                               func=AF.Exp, scale=0.25)
            P.sS.inc(ins)
    v.wait_ge(P.sS.h, P.sS.n)
    v.reciprocal(out=sc["srow"][0:1, 3:4], in_=sc["srow"][0:1, 2:3])
    ins = v.tensor_scalar(out=sc["srow"][0:1, 3:4], in0=sc["srow"][0:1, 3:4],
                          scalar1=0.99, scalar2=1.0, op0=OP.mult, op1=OP.min)
    P.sV.inc(ins)
    t.wait_ge(P.sV.h, P.sV.n)
    ins = t.matmul(sc["pSc"][0:16, 3:4], P.ones_row[0:1, 0:16], sc["srow"][0:1, 3:4],
                   start=True, stop=True)
    P.sT.inc(ins)
    s.wait_ge(P.sT.h, P.sT.n)
    ins = s.copy(out=sc["svec"][:, 2:3], in_=sc["pSc"][0:16, 3:4])
    P.sS.inc(ins)
    v.wait_ge(P.sS.h, P.sS.n)
    An_t = sc["m16"][:, 48:64]
    AnT_t = sc["m16"][:, 64:80]
    v.tensor_scalar(out=An_t, in0=A_t, scalar1=sc["svec"][:, 2:3], scalar2=None, op0=OP.mult)
    ins = v.tensor_scalar(out=AnT_t, in0=AT_t, scalar1=sc["svec"][:, 2:3], scalar2=None, op0=OP.mult)
    P.sV.inc(ins)

    E1T = sc["eep"][:, 0:16]
    E2T = sc["eep"][:, 32:48]
    ins = v.memset(sc["eep"][:, 16:32], 0.0)
    v.memset(sc["eep"][:, 48:64], 0.0)
    ins = v.tensor_scalar(out=E1T, in0=C16, scalar1=col(64, 16), scalar2=None, op0=OP.mult)
    P.sV.inc(ins)
    t.wait_ge(P.sV.h, P.sV.n)
    t.wait_ge(P.sS.h, P.sS.n)
    ins = t.matmul(sc["pB1"][0:16, 0:16], An_t, C16, start=True, stop=True)
    P.sT.inc(ins)
    v.wait_ge(P.sT.h, P.sT.n)
    ins = v.tensor_scalar(out=E2T, in0=sc["pB1"][0:16, 0:16], scalar1=col(64, 16),
                          scalar2=None, op0=OP.mult)
    P.sV.inc(ins)
    t.wait_ge(P.sV.h, P.sV.n)
    ins = t.matmul(sc["pSc"][0:16, 0:1], AnT_t, col(65, 16), start=True, stop=True)
    P.sT.inc(ins)
    s.wait_ge(P.sT.h, P.sT.n)
    ins = s.copy(out=sc["svec"][:, 3:4], in_=sc["pSc"][0:16, 0:1])
    P.sS.inc(ins)
    t.wait_ge(P.sS.h, P.sS.n)
    ins = t.matmul(sc["pSc"][0:16, 1:2], C16, sc["svec"][:, 3:4], start=True, stop=True)
    ins2 = t.matmul(sc["pSc"][0:16, 2:3], C16, col(65, 16), start=True, stop=True)
    P.sT.inc(ins2)
    s.wait_ge(P.sT.h, P.sT.n)
    s.copy(out=sc["svec"][:, 4:5], in_=sc["pSc"][0:16, 1:2])
    ins = s.copy(out=sc["svec"][:, 5:6], in_=sc["pSc"][0:16, 2:3])
    P.sS.inc(ins)
    v.wait_ge(P.sS.h, P.sS.n)
    ins = v.tensor_scalar(out=sc["rtraj"][:, :], in0=sc["dtb16"][:, :],
                          scalar1=sc["svec"][:, 4:5], scalar2=sc["svec"][:, 5:6],
                          op0=OP.mult, op1=OP.add)
    P.sV.inc(ins)
    ins = v.tensor_tensor(out=sc["F"][:, :], in0=sc["F"][:, :], in1=sc["rtraj"][:, :], op=OP.add)
    P.sV.inc(ins)
    f_done_v = P.sV.n

    for e in ("tensor", "scalar"):
        P.eng[e].wait_ge(P.sV.h, f_done_v)
    vcomb = sc["svec"][:, 6:8]
    prev_act = [0, 0]
    for k in range(1, T):
        par = k % 2
        t.wait_ge(P.sS.h, P.sS.n)
        t.wait_ge(P.sV.h, P.sV.n)
        cprev = sc["ctraj"][:, k - 1:k]
        t.matmul(sc["pSc"][0:16, 0:1], cprev.to_broadcast([16, 16]), cprev, start=True, stop=True)
        t.matmul(sc["pSc"][0:16, 1:2], sc["eep"][:, 0:16], cprev, start=True, stop=True)
        ins = t.matmul(sc["pSc"][0:16, 2:3], sc["eep"][:, 32:48], cprev, start=True, stop=True)
        P.sT.inc(ins)
        v.wait_ge(P.sT.h, P.sT.n)
        v.wait_ge(P.sS.h, prev_act[par])
        v.tensor_scalar(out=sc["svec"][:, 5:6], in0=sc["pSc"][0:16, 2:3],
                        scalar1=sc["dtb16"][:, k:k + 1], scalar2=None, op0=OP.mult)
        ins = v.tensor_tensor(out=vcomb[:, par:par + 1], in0=sc["svec"][:, 5:6],
                              in1=sc["pSc"][0:16, 1:2], op=OP.add)
        P.sV.inc(ins)
        comb_n = P.sV.n
        s.wait_ge(P.sT.h, P.sT.n)
        s.activation(out=sc["sl1"][:, 0:1], in_=sc["pSc"][0:16, 0:1], func=AF.Ln,
                     scale=0.0625, bias=1e-5)
        s.wait_ge(P.sS.h, P.sS.n)
        s.activation(out=sc["rtraj"][:, k - 1:k], in_=sc["sl1"][:, 0:1], func=AF.Exp, scale=-0.5)
        s.wait_ge(P.sS.h, P.sS.n)
        s.wait_ge(P.sV.h, comb_n)
        ins = s.activation(out=sc["ctraj"][:, k:k + 1], in_=vcomb[:, par:par + 1], func=AF.Identity,
                           scale=sc["rtraj"][:, k - 1:k], bias=sc["F"][:, k:k + 1])
        P.sS.inc(ins)
        prev_act[par] = P.sS.n
    t.wait_ge(P.sS.h, P.sS.n)
    cprev = sc["ctraj"][:, T - 1:T]
    ins = t.matmul(sc["pSc"][0:16, 0:1], cprev.to_broadcast([16, 16]), cprev, start=True, stop=True)
    P.sT.inc(ins)
    s.wait_ge(P.sT.h, P.sT.n)
    s.activation(out=sc["sl1"][:, 0:1], in_=sc["pSc"][0:16, 0:1], func=AF.Ln,
                 scale=0.0625, bias=1e-5)
    s.wait_ge(P.sS.h, P.sS.n)
    ins = s.activation(out=sc["rtraj"][:, T - 1:T], in_=sc["sl1"][:, 0:1], func=AF.Exp, scale=-0.5)
    P.sS.inc(ins)
    v.wait_ge(P.sS.h, P.sS.n)
    ins = v.tensor_tensor(out=sc["rtraj"][:, :], in0=sc["ctraj"][:, :], in1=sc["rtraj"][:, :], op=OP.mult)
    P.sV.inc(ins)
    ins = v.tensor_scalar(out=sc["states"][:, :], in0=sc["rtraj"][:, :],
                          scalar1=col(64, 16), scalar2=col(65, 16), op0=OP.mult, op1=OP.add)
    P.sV.inc(ins)
    states_done = P.sV.n

    g.wait_ge(P.sS.h, P.sS.n)
    g.wait_ge(P.sV.h, P.sV.n)
    ins = g.iota(sc["iotts"][:, :], pattern=[[1, 128]], base=0, channel_multiplier=-1,
                 allow_small_or_imprecise_dtypes=True)
    P.sG.inc(ins)
    v.wait_ge(P.sG.h, P.sG.n)
    v.tensor_scalar(out=sc["caus01"][:, :], in0=sc["iotts"][:, :], scalar1=0.0,
                    scalar2=None, op0=OP.is_ge)
    ins = v.tensor_scalar(out=sc["iotts"][:, :], in0=sc["iotts"][:, :], scalar1=0.0,
                          scalar2=None, op0=OP.max)
    P.sV.inc(ins)

    wq_free = P.sT.n
    for hp in range(8):
        wpar = hp % 2
        sw = P.sWp[wpar]
        g.wait_ge(P.sT.h, wq_free)
        for si, nm in enumerate(("l1a_wkT", "l1a_wrT", "l1a_wvT")):
            ins = g.dma_start(
                out=sc["wqkv"][wpar][:, :].rearrange("p (k m) -> p k m", k=KC)[:, :, si * 128:(si + 1) * 128],
                in_=wdram[nm][:, hp * 128:(hp + 1) * 128].rearrange("(k p) m -> p k m", p=128))
            sw.inc(ins, 16)
        w_ready = sw.n
        t.wait_ge(sw.h, w_ready)
        t.wait_ge(P.sV.h, xln_done)
        for nb in range(NT):
            xv = [sc["xln"][k][:, nb * 512:(nb + 1) * 512] for k in range(KC)]
            t.wait_ge(P.sV.h, P.sV.n)
            t.wait_ge(P.sS.h, P.sS.n)
            emit_ln(P, xv, [xx[:, :] for xx in sc["lnh"]], 512, sc,
                    affine=([C[0:128, 16 + k:17 + k] for k in range(KC)],
                            [C[0:128, 24 + k:25 + k] for k in range(KC)]))
            t.wait_ge(P.sV.h, P.sV.n)
            for io, dstname in ((0, "kT"), (1, "rT")):
                par = io
                t.wait_ge(P.sS.h, P.sS.n)
                pp = sc["pA"] if par == 0 else sc["pB1"]
                for k in range(KC):
                    ins = t.matmul(pp[:, 0:512],
                                   sc["wqkv"][wpar][:, k * 384 + io * 128:k * 384 + io * 128 + 128],
                                   sc["lnh"][k][:, :], start=(k == 0), stop=(k == KC - 1))
                P.sT.inc(ins)
                if io == 0:
                    v.wait_ge(P.sT.h, P.sT.n)
                    ins = v.tensor_scalar(out=sc["kT"][:, nb * 512:(nb + 1) * 512], in0=pp[:, 0:512],
                                          scalar1=C[0:128, 56 + hp:57 + hp], scalar2=None, op0=OP.mult)
                    P.sV.inc(ins)
                else:
                    s.wait_ge(P.sT.h, P.sT.n)
                    ins = s.activation(out=sc["rT"][:, nb * 512:(nb + 1) * 512], in_=pp[:, 0:512],
                                       func=AF.Sigmoid)
                    P.sS.inc(ins)
            for tsub in range(4):
                ti = nb * 4 + tsub
                par = tsub % 2
                pp = sc["pA"] if par == 0 else sc["pB1"]
                t.wait_ge(P.sS.h, P.sS.n)
                t.wait_ge(P.sV.h, P.sV.n)
                for k in range(KC):
                    ins = t.matmul(pp[:, 0:128],
                                   sc["lnh"][k][:, tsub * 128:(tsub + 1) * 128],
                                   sc["wqkv"][wpar][:, k * 384 + 256:k * 384 + 384],
                                   start=(k == 0), stop=(k == KC - 1))
                P.sT.inc(ins)
                s.wait_ge(P.sT.h, P.sT.n)
                ins = s.copy(out=sc["vtm"][:, ti * 128:(ti + 1) * 128], in_=pp[:, 0:128])
                P.sS.inc(ins)
        s.wait_ge(P.sO.h, P.sO.n)
        v.wait_ge(P.sT.h, P.sT.n)
        ins = v.tensor_scalar(out=sc["rev127"], in0=sc["iotts"][0:1, :],
                              scalar1=-1.0, scalar2=127.0, op0=OP.mult, op1=OP.add)
        P.sV.inc(ins)

        for side in range(2):
            h = hp * 2 + side
            b0 = 64 * side
            dcol = C[0:64, 67 + h:68 + h]
            v.wait_ge(P.sS.h, P.sS.n)
            v.memset(sc["zer"][:, :], 0.0)
            ins = v.reciprocal(out=sc["rd64"][:, 0:1], in_=dcol)
            P.sV.inc(ins)
            ins = v.tensor_tensor_scan(out=sc["PW"][:, 0:129], data0=dcol.to_broadcast([64, 129]),
                                       data1=sc["zer"][:, 0:129], initial=sc["rd64"][:, 0:1],
                                       op0=OP.mult, op1=OP.add)
            P.sV.inc(ins)
            sy.wait_ge(P.sV.h, P.sV.n)
            sy.wait_ge(P.sS.h, P.sS.n)
            ins = sy.dma_start(out=sc["rows"][0:1, 256:320], in_=wdram["l1a_drow"][h:h + 1, :])
            P.sI.inc(ins, 16)
            s.wait_ge(P.sI.h, P.sI.n)
            s.wait_ge(P.sT.h, P.sT.n)
            ins = s.activation(out=sc["drow"][0:1, 64:128], in_=sc["rows"][0:1, 256:320],
                               func=AF.Ln)
            P.sS.inc(ins)
            s.wait_ge(P.sS.h, P.sS.n)
            ins = s.copy(out=sc["srow"][0:1, 4:5], in_=sc["drow"][0:1, 64:65])
            P.sS.inc(ins)
            t.wait_ge(P.sS.h, P.sS.n)
            ins = t.matmul(sc["pA"][:, 0:64], sc["rev127"][0:1, 0:128], sc["drow"][0:1, 64:128],
                           start=True, stop=True)
            P.sT.inc(ins)
            s.wait_ge(P.sT.h, P.sT.n)
            ins = s.activation(out=sc["PWrevT"][:, :], in_=sc["pA"][:, 0:64], func=AF.Exp)
            P.sS.inc(ins)
            ins = t.matmul(sc["pA"][:, 0:1], P.ones_row[0:1, 0:128], sc["srow"][0:1, 4:5],
                           start=True, stop=True)
            P.sT.inc(ins)
            v.wait_ge(P.sT.h, P.sT.n)
            ins = v.tensor_copy(out=sc["lndB"][:, 0:1], in_=sc["pA"][:, 0:1])
            P.sV.inc(ins)
            s.wait_ge(P.sV.h, P.sV.n)
            ins = s.activation(out=sc["m1t"][:, :], in_=sc["iotts"][:, :], func=AF.Exp,
                               scale=sc["lndB"][:, 0:1])
            P.sS.inc(ins)
            v.wait_ge(P.sS.h, P.sS.n)
            ins = v.tensor_tensor(out=sc["maskh"][:, :], in0=sc["m1t"][:, :],
                                  in1=sc["caus01"][:, :], op=OP.mult)
            P.sV.inc(ins)
            ins = v.memset(sc["Wst"][:, :], 0.0)
            P.sV.inc(ins)
            prep_done = P.sV.n
            t.wait_ge(P.sV.h, prep_done)
            for c in range(T // 128):
                cs = c * 128
                t.wait_ge(P.sV.h, P.sV.n)
                t.wait_ge(P.sS.h, P.sS.n)
                ins = t.matmul(sc["pA"][:, 0:128], sc["kT"][b0:b0 + 64, cs:cs + 128],
                               sc["rT"][b0:b0 + 64, cs:cs + 128], start=True, stop=True)
                P.sT.inc(ins)
                v.wait_ge(P.sT.h, P.sT.n)
                ins = v.tensor_tensor(out=sc["Amt"][:, :], in0=sc["pA"][:, 0:128],
                                      in1=sc["maskh"][:, :], op=OP.mult)
                P.sV.inc(ins)
                if b0 == 0:
                    ins = v.tensor_tensor(out=sc["rpb"][:, :], in0=sc["rT"][0:64, cs:cs + 128],
                                          in1=sc["PW"][:, 1:129], op=OP.mult)
                    P.sV.inc(ins)
                else:
                    ins = t.matmul(sc["pA"][0:64, 128:256], sc["identbb"][b0:b0 + 64, b0:b0 + 64],
                                   sc["rT"][b0:b0 + 64, cs:cs + 128], start=True, stop=True)
                    P.sT.inc(ins)
                    v.wait_ge(P.sT.h, P.sT.n)
                    ins = v.tensor_tensor(out=sc["rpb"][:, :], in0=sc["pA"][0:64, 128:256],
                                          in1=sc["PW"][:, 1:129], op=OP.mult)
                    P.sV.inc(ins)
                ins = v.tensor_copy(out=sc["Wb"][:, :], in_=sc["Wst"][:, :])
                P.sV.inc(ins)
                t.wait_ge(P.sV.h, P.sV.n)
                t.matmul(sc["pY"][:, 0:128], sc["Wb"][:, :], sc["rpb"][:, :], start=True, stop=False)
                ins = t.matmul(sc["pY"][:, 0:128], sc["vtm"][:, cs + b0:cs + b0 + 64], sc["Amt"][:, :],
                               start=False, stop=True)
                P.sT.inc(ins)
                s.wait_ge(P.sT.h, P.sT.n)
                yv = sc["xsl"][cs // 512]
                ins = s.activation(out=yv[b0:b0 + 64, (cs % 512):(cs % 512) + 128],
                                   in_=sc["pY"][:, 0:128], func=AF.Identity, scale=3.0 / 7.0)
                P.sS.inc(ins)
                ins = t.transpose(sc["pKT"][:, 0:64], sc["kT"][b0:b0 + 64, cs:cs + 128],
                                  sc["identbb"][b0:b0 + 64, b0:b0 + 64])
                P.sT.inc(ins)
                s.wait_ge(P.sT.h, P.sT.n)
                ins = s.copy(out=sc["ktm"][:, :], in_=sc["pKT"][:, 0:64])
                P.sS.inc(ins)
                v.wait_ge(P.sS.h, P.sS.n)
                ins = v.tensor_tensor(out=sc["khat"][:, :], in0=sc["ktm"][:, :],
                                      in1=sc["PWrevT"][:, :], op=OP.mult)
                P.sV.inc(ins)
                t.wait_ge(P.sV.h, P.sV.n)
                ins = t.matmul(sc["pU"][:, 0:64], sc["khat"][:, :], sc["vtm"][:, cs + b0:cs + b0 + 64],
                               start=True, stop=True)
                P.sT.inc(ins)
                v.wait_ge(P.sT.h, P.sT.n)
                ins = v.scalar_tensor_tensor(out=sc["Wst"][:, :], in0=sc["Wst"][:, :],
                                             scalar=sc["PW"][:, 128:129], in1=sc["pU"][:, 0:64],
                                             op0=OP.mult, op1=OP.add)
                P.sV.inc(ins)
        sy.wait_ge(P.sS.h, P.sS.n)
        for kk in range(NT):
            ins = sy.dma_start(out=wdram["l1a_ydram"][hp * 128:(hp + 1) * 128, kk * 512:(kk + 1) * 512],
                               in_=sc["xsl"][kk][:, :])
            P.sO.inc(ins, 16)
        wq_free = P.sT.n



    g.wait_ge(P.sT.h, P.sT.n)
    ins = g.dma_start(out=sc["wo0"][:, 0:4096].rearrange("p (k m) -> p k m", k=4),
                      in_=wdram["l1a_woT"][:, :].rearrange("(k p) m -> p k m", p=128)[:, 0:4, :])
    P.sB.inc(ins, 16)
    ins = g.dma_start(out=sc["wo1"][:, 0:4096].rearrange("p (k m) -> p k m", k=4),
                      in_=wdram["l1a_woT"][:, :].rearrange("(k p) m -> p k m", p=128)[:, 4:8, :])
    P.sB.inc(ins, 16)
    ins = g.dma_start(out=sc["CwTb"][:, :].rearrange("p (a m) -> p a m", a=1),
                      in_=wdram["l1a_CwT"][:, :].rearrange("(a p) m -> p a m", p=16))
    P.sB.inc(ins, 16)
    for e in ("tensor",):
        P.eng[e].wait_ge(P.sB.h, P.sB.n)

    sc["comb"] = sc["lnh"]
    sc["addl"] = sc["yslab"]

    def wo_chunk(k, f):
        tile = sc["wo0"] if k < 4 else sc["wo1"]
        return tile[:, (k % 4) * 1024 + f * 128:(k % 4) * 1024 + f * 128 + 128]

    sy.wait_ge(P.sO.h, P.sO.n)
    g.wait_ge(P.sO.h, P.sO.n)
    for sl in range(NT):
        g.wait_ge(P.sT.h, P.yslab_gate)
        g.wait_ge(P.sV.h, P.yslab_gateV)
        for k in range(KC):
            ins = g.dma_start(out=sc["yslab"][k][:, :],
                              in_=wdram["l1a_ydram"][k * 128:(k + 1) * 128, sl * 512:(sl + 1) * 512])
            P.sWp[0].inc(ins, 16)
        for e in ("tensor",):
            P.eng[e].wait_ge(P.sWp[0].h, P.sWp[0].n)
        xv = [sc["xln"][k][:, sl * 512:(sl + 1) * 512] for k in range(KC)]
        t.wait_ge(P.sV.h, P.sV.n)
        t.wait_ge(P.sS.h, P.sS.n)
        emit_ln(P, xv, [xx[:, :] for xx in sc["xsl"]], 512, sc,
                affine=([C[0:128, 0 + k:1 + k] for k in range(KC)],
                        [C[0:128, 8 + k:9 + k] for k in range(KC)]))
        for f in range(KC):
            t.wait_ge(P.sV.h, P.sV.n)
            t.wait_ge(P.sS.h, P.sS.n)
            for k in range(KC):
                t.matmul(sc["pA"][:, 0:512], wo_chunk(k, f),
                         sc["yslab"][k][:, :], start=(k == 0), stop=False)
            ins = t.matmul(sc["pA"][:, 0:512], sc["CwTb"][:, f * 128:(f + 1) * 128],
                           sc["states"][:, sl * 512:(sl + 1) * 512], start=False, stop=True)
            P.sT.inc(ins)
            v.wait_ge(P.sT.h, P.sT.n)
            v.tensor_scalar(out=sc["tmp"][:, :], in0=sc["xsl"][f][:, :],
                            scalar1=C[0:128, 48 + f:49 + f], scalar2=None, op0=OP.mult)
            ins = v.tensor_tensor(out=sc["comb"][f][:, :], in0=sc["pA"][:, 0:512],
                                  in1=sc["tmp"][:, :], op=OP.add)
            P.sV.inc(ins)
        t.wait_ge(P.sV.h, P.sV.n)
        t.wait_ge(P.sS.h, P.sS.n)
        emit_ln(P, [xx[:, :] for xx in sc["comb"]], [xx[:, :] for xx in sc["addl"]], 512, sc,
                affine=([C[0:128, 32 + k:33 + k] for k in range(KC)],
                        [C[0:128, 40 + k:41 + k] for k in range(KC)]))
        for f in range(KC):
            par = f % 2
            sy.wait_ge(P.sV.h, P.xp_cons[par])
            ins = sy.dma_start(out=sc["xb"][par][:, :],
                               in_=xdram[f * 128:(f + 1) * 128, sl * 512:(sl + 1) * 512])
            P.sXp[par].inc(ins, 16)
            v.wait_ge(P.sXp[par].h, P.sXp[par].n)
            v.wait_ge(P.sO.h, P.sO.n)
            v.tensor_tensor(out=sc["tmp"][:, :], in0=sc["xln"][f][:, sl * 512:(sl + 1) * 512],
                            in1=sc["addl"][f][:, :], op=OP.add)
            ins = v.tensor_tensor(out=sc["xb"][par][:, :], in0=sc["xb"][par][:, :],
                                  in1=sc["tmp"][:, :], op=OP.add)
            P.sV.inc(ins)
            P.xp_cons[par] = P.sV.n
            sy.wait_ge(P.sV.h, P.sV.n)
            ins = sy.dma_start(out=xdram[f * 128:(f + 1) * 128, sl * 512:(sl + 1) * 512],
                               in_=sc["xb"][par][:, :])
            P.sO.inc(ins, 16)
        P.yslab_gate = P.sT.n
        P.yslab_gateV = P.sV.n


def build_l1a_test_program(T):
    nc = bass.Bass("TRN2", target_bir_lowering=False)
    xin = nc.dram_tensor("xin", [D, T], F32, kind="ExternalInput").ap()
    xout = nc.dram_tensor("xout", [D, T], F32, kind="ExternalOutput").ap()
    ident_in = nc.dram_tensor("ident_in", [128, 128], F32, kind="ExternalInput").ap()
    wdram = {}
    for nm in ("l1a_wkT", "l1a_wvT", "l1a_wrT", "l1a_woT"):
        wdram[nm] = nc.dram_tensor(nm, [D, D], F32, kind="ExternalInput").ap()
    wdram["l1a_BwT"] = nc.dram_tensor("l1a_BwT", [D, 16], F32, kind="ExternalInput").ap()
    wdram["l1a_dtwT"] = nc.dram_tensor("l1a_dtwT", [D, 1], F32, kind="ExternalInput").ap()
    wdram["l1a_CwT"] = nc.dram_tensor("l1a_CwT", [16, D], F32, kind="ExternalInput").ap()
    wdram["l1a_AAT"] = nc.dram_tensor("l1a_AAT", [32, 16], F32, kind="ExternalInput").ap()
    wdram["l1a_cols"] = nc.dram_tensor("l1a_cols", [128, 84], F32, kind="ExternalInput").ap()
    wdram["l1a_drow"] = nc.dram_tensor("l1a_drow", [16, 64], F32, kind="ExternalInput").ap()
    wdram["l1a_ydram"] = nc.dram_tensor("l1a_ydram", [D, T], BF16, kind="Internal").ap()
    with ExitStack() as ctx:
        P = Prog(nc, ctx)
        sc = alloc_l1a_scratch(P, ctx, T)
        common_init(P, sc, ident_in, [], ())
        sy = P.eng["sync"]
        ins = sy.dma_start(out=xout[:, :], in_=xin[:, :])
        P.sX.inc(ins, 16)
        sy.wait_ge(P.sX.h, P.sX.n)
        emit_l1attn_block(P, xout, wdram, T, sc)
        sy.wait_ge(P.sO.h, P.sO.n)
        block = ctx.enter_context(nc.Block())
        P.finalize(block)
    return nc


def pack_l1a_cols(time_decay, time_first, D_vec, g_ssm, b_ssm, g_rwkv, b_rwkv,
                  g_out, b_out, sn_g, sn_b, dt_b):
    cols = np.zeros((128, 84), np.float32)
    def putv(off, vec):
        cols[:, off:off + 8] = np.asarray(vec).reshape(8, 128).T
    putv(0, g_ssm); putv(8, b_ssm); putv(16, g_rwkv); putv(24, b_rwkv)
    putv(32, g_out); putv(40, b_out); putv(48, D_vec)
    putv(56, np.asarray(time_first).reshape(-1))
    cols[0:16, 64] = np.asarray(sn_g)
    cols[0:16, 65] = np.asarray(sn_b)
    cols[0, 66] = float(np.asarray(dt_b).reshape(-1)[0])
    cols[0:64, 67:83] = np.asarray(time_decay).T
    return cols


def hT(a):
    return np.ascontiguousarray(np.asarray(a).T)


def hC(a):
    return np.ascontiguousarray(np.asarray(a))


def build_ffn_test_program(which, T):
    nc = bass.Bass("TRN2", target_bir_lowering=False)
    xin = nc.dram_tensor("xin", [D, T], F32, kind="ExternalInput").ap()
    xout = nc.dram_tensor("xout", [D, T], F32, kind="ExternalOutput").ap()
    ident_in = nc.dram_tensor("ident_in", [128, 128], F32, kind="ExternalInput").ap()
    wdram = {}
    if which == "l3f":
        wdram["l3f_gT"] = nc.dram_tensor("l3f_gT", [D, 3072], I32, kind="ExternalInput").ap()
        wdram["l3f_uT"] = nc.dram_tensor("l3f_uT", [D, 3072], I32, kind="ExternalInput").ap()
        wdram["l3f_dT"] = nc.dram_tensor("l3f_dT", [3072, D], I32, kind="ExternalInput").ap()
    elif which == "l0f":
        wdram["l0f_router"] = nc.dram_tensor("l0f_router", [D, 8], F32, kind="ExternalInput").ap()
        wdram["l0f_gT"] = [nc.dram_tensor(f"l0f_gT{e}", [D, 512], F32, kind="ExternalInput").ap() for e in range(8)]
        wdram["l0f_uT"] = [nc.dram_tensor(f"l0f_uT{e}", [D, 512], F32, kind="ExternalInput").ap() for e in range(8)]
        wdram["l0f_dT"] = [nc.dram_tensor(f"l0f_dT{e}", [512, D], F32, kind="ExternalInput").ap() for e in range(8)]
        wdram["l0f_shgT"] = nc.dram_tensor("l0f_shgT", [D, 1024], F32, kind="ExternalInput").ap()
        wdram["l0f_shuT"] = nc.dram_tensor("l0f_shuT", [D, 1024], F32, kind="ExternalInput").ap()
        wdram["l0f_shdT"] = nc.dram_tensor("l0f_shdT", [1024, D], F32, kind="ExternalInput").ap()
    elif which == "l1f":
        wdram["l1f_router"] = nc.dram_tensor("l1f_router", [D, 16], F32, kind="ExternalInput").ap()
        wdram["l1f_ebias"] = nc.dram_tensor("l1f_ebias", [1, 16], F32, kind="ExternalInput").ap()
        wdram["l1f_gT"] = [nc.dram_tensor(f"l1f_gT{e}", [D, 512], F32, kind="ExternalInput").ap() for e in range(16)]
        wdram["l1f_uT"] = [nc.dram_tensor(f"l1f_uT{e}", [D, 512], F32, kind="ExternalInput").ap() for e in range(16)]
        wdram["l1f_dT"] = [nc.dram_tensor(f"l1f_dT{e}", [512, D], F32, kind="ExternalInput").ap() for e in range(16)]
        wdram["l1f_shgT"] = nc.dram_tensor("l1f_shgT", [D, 1024], F32, kind="ExternalInput").ap()
        wdram["l1f_shuT"] = nc.dram_tensor("l1f_shuT", [D, 1024], F32, kind="ExternalInput").ap()
        wdram["l1f_shdT"] = nc.dram_tensor("l1f_shdT", [1024, D], F32, kind="ExternalInput").ap()
    elif which == "l2f":
        wdram["l2f_router"] = nc.dram_tensor("l2f_router", [D, 7], F32, kind="ExternalInput").ap()
        wdram["l2f_gT"] = [nc.dram_tensor(f"l2f_gT{e}", [D, 4096], F32, kind="ExternalInput").ap() for e in range(7)]
        wdram["l2f_uT"] = [nc.dram_tensor(f"l2f_uT{e}", [D, 4096], F32, kind="ExternalInput").ap() for e in range(7)]
        wdram["l2f_dT"] = [nc.dram_tensor(f"l2f_dT{e}", [4096, D], F32, kind="ExternalInput").ap() for e in range(7)]
        wdram["l2f_shgT"] = nc.dram_tensor("l2f_shgT", [D, 4096], F32, kind="ExternalInput").ap()
        wdram["l2f_shuT"] = nc.dram_tensor("l2f_shuT", [D, 4096], F32, kind="ExternalInput").ap()
        wdram["l2f_shdT"] = nc.dram_tensor("l2f_shdT", [4096, D], F32, kind="ExternalInput").ap()

    with ExitStack() as ctx:
        P = Prog(nc, ctx)
        Th = min(T, 2048)
        sc = alloc_ffn_scratch(P, ctx, Th)
        sl = []
        if which == "l1f":
            sl.append((sc["ebrow"][0:1, 0:16], wdram["l1f_ebias"][0:1, :]))
        ritems = []
        if which == "l0f":
            ritems.append((wdram["l0f_router"], "rw_l0", 8))
        elif which == "l1f":
            ritems.append((wdram["l1f_router"], "rw_l1", 16))
        elif which == "l2f":
            ritems.append((wdram["l2f_router"], "rw_l2", 7))
        common_init(P, sc, ident_in, sl, ritems)
        sy = P.eng["sync"]
        ins = sy.dma_start(out=xout[:, :], in_=xin[:, :])
        P.sX.inc(ins, 16)
        sy.wait_ge(P.sX.h, P.sX.n)
        emit_ffn_block(P, which, xout, wdram, T, sc)
        sy.wait_ge(P.sO.h, P.sO.n)
        block = ctx.enter_context(nc.Block())
        P.finalize(block)
    return nc


def build_attn_test_program(which, T):
    nc = bass.Bass("TRN2", target_bir_lowering=False)
    xin = nc.dram_tensor("xin", [D, T], F32, kind="ExternalInput").ap()
    xout = nc.dram_tensor("xout", [D, T], F32, kind="ExternalOutput").ap()
    ident_in = nc.dram_tensor("ident_in", [128, 128], F32, kind="ExternalInput").ap()
    wdram = {}
    sl = []
    if which == "l0a":
        wdram["l0a_wqkv"] = nc.dram_tensor("l0a_wqkv", [D, 3072], F32, kind="ExternalInput").ap()
        wdram["l0a_wproj"] = nc.dram_tensor("l0a_wproj", [D, D], F32, kind="ExternalInput").ap()
        wdram["l0a_bias"] = nc.dram_tensor("l0a_bias", [1, 4096], F32, kind="ExternalInput").ap()
    else:
        for nm in ("wq", "wk", "wv", "wo"):
            wdram[f"l3a_{nm}"] = nc.dram_tensor(f"l3a_{nm}", [D, D], F32, kind="ExternalInput").ap()
    with ExitStack() as ctx:
        P = Prog(nc, ctx)
        sc = alloc_attn_scratch(P, ctx, T)
        common_init(P, sc, ident_in, sl, ())
        sy = P.eng["sync"]
        ins = sy.dma_start(out=xout[:, :], in_=xin[:, :])
        P.sX.inc(ins, 16)
        sy.wait_ge(P.sX.h, P.sX.n)
        emit_attn_block(P, which, xout, wdram, T, sc)
        sy.wait_ge(P.sO.h, P.sO.n)
        block = ctx.enter_context(nc.Block())
        P.finalize(block)
    return nc


def build_l2a_test_program(T, n_cores):
    nc = bass.Bass("TRN2", target_bir_lowering=False)
    xin = nc.dram_tensor("xin", [D, T], F32, kind="ExternalInput").ap()
    xout = nc.dram_tensor("xout", [D, T], F32, kind="ExternalOutput").ap()
    ident_in = nc.dram_tensor("ident_in", [128, 128], F32, kind="ExternalInput").ap()
    mask_in = nc.dram_tensor("l2a_mask", [128, 128], F32, kind="ExternalInput").ap()
    bsel_in = nc.dram_tensor("l2a_bsel", [128, 128], F32, kind="ExternalInput").ap()
    contrib = nc.dram_tensor("l2a_contrib", [T, D], F32, kind="Internal").ap()
    gath = nc.dram_tensor("l2a_gath", [n_cores * T, D], F32, kind="Internal").ap()
    gath2 = nc.dram_tensor("l2a_gath2", [n_cores * T, D], F32, kind="Internal").ap()
    myout = nc.dram_tensor("l2a_myout", [T, D], F32, kind="Internal").ap()
    with ExitStack() as ctx:
        P = Prog(nc, ctx)
        sc = alloc_l2a_scratch(P, ctx, T)
        common_init(P, sc, ident_in, [], ())
        sy = P.eng["sync"]
        ins = sy.dma_start(out=xout[:, :], in_=xin[:, :])
        P.sX.inc(ins, 16)
        sy.wait_ge(P.sX.h, P.sX.n)
        emit_l2attn_block(P, xout, T, sc, contrib, gath, gath2, myout, mask_in, bsel_in, n_cores)
        sy.wait_ge(P.sO.h, P.sO.n)
        block = ctx.enter_context(nc.Block())
        P.finalize(block)
    return nc


class _UniqNC:

    def __init__(self, nc):
        self._nc = nc
        self._names = {}

    def __getattr__(self, k):
        return getattr(self._nc, k)

    def _u(self, name):
        i = self._names.get(name, 0)
        self._names[name] = i + 1
        return name if i == 0 else f"{name}_p{i}"

    def sbuf_tensor(self, name, *a, **k):
        return self._nc.sbuf_tensor(self._u(name), *a, **k)

    def psum_tensor(self, name, *a, **k):
        return self._nc.psum_tensor(self._u(name), *a, **k)

    def alloc_sbuf_tensor_at(self, name, *a, **k):
        return self._nc.alloc_sbuf_tensor_at(self._u(name), *a, **k)

    def semaphore(self, name, *a, **k):
        return self._nc.semaphore(self._u(name), *a, **k)


def build_fused_program(T, n_cores):
    nc = bass.Bass("TRN2", target_bir_lowering=False)
    xin = nc.dram_tensor("xin", [D, T], F32, kind="ExternalInput").ap()
    xout = nc.dram_tensor("xout", [D, T], F32, kind="ExternalOutput").ap()
    ident_in = nc.dram_tensor("ident_in", [128, 128], F32, kind="ExternalInput").ap()
    wd = {}
    wd["l0f_router"] = nc.dram_tensor("l0f_router", [D, 8], F32, kind="ExternalInput").ap()
    wd["l0f_gT"] = [nc.dram_tensor(f"l0f_gT{e}", [D, 512], F32, kind="ExternalInput").ap() for e in range(8)]
    wd["l0f_uT"] = [nc.dram_tensor(f"l0f_uT{e}", [D, 512], F32, kind="ExternalInput").ap() for e in range(8)]
    wd["l0f_dT"] = [nc.dram_tensor(f"l0f_dT{e}", [512, D], F32, kind="ExternalInput").ap() for e in range(8)]
    for nm in ("l0f_shgT", "l0f_shuT", "l0f_shdT"):
        wd[nm] = nc.dram_tensor(nm, [D, D], F32, kind="ExternalInput").ap()
    wd["l0a_wqkv"] = nc.dram_tensor("l0a_wqkv", [D, 3072], F32, kind="ExternalInput").ap()
    wd["l0a_wproj"] = nc.dram_tensor("l0a_wproj", [D, D], F32, kind="ExternalInput").ap()
    wd["l0a_bias"] = nc.dram_tensor("l0a_bias", [1, 4096], F32, kind="ExternalInput").ap()
    wd["l1f_router"] = nc.dram_tensor("l1f_router", [D, 16], F32, kind="ExternalInput").ap()
    wd["l1f_ebias"] = nc.dram_tensor("l1f_ebias", [1, 16], F32, kind="ExternalInput").ap()
    wd["l1f_gT"] = [nc.dram_tensor(f"l1f_gT{e}", [D, 512], F32, kind="ExternalInput").ap() for e in range(16)]
    wd["l1f_uT"] = [nc.dram_tensor(f"l1f_uT{e}", [D, 512], F32, kind="ExternalInput").ap() for e in range(16)]
    wd["l1f_dT"] = [nc.dram_tensor(f"l1f_dT{e}", [512, D], F32, kind="ExternalInput").ap() for e in range(16)]
    for nm in ("l1f_shgT", "l1f_shuT", "l1f_shdT"):
        wd[nm] = nc.dram_tensor(nm, [D, D], F32, kind="ExternalInput").ap()
    for nm in ("l1a_wkT", "l1a_wvT", "l1a_wrT", "l1a_woT"):
        wd[nm] = nc.dram_tensor(nm, [D, D], F32, kind="ExternalInput").ap()
    wd["l1a_BwT"] = nc.dram_tensor("l1a_BwT", [D, 16], F32, kind="ExternalInput").ap()
    wd["l1a_dtwT"] = nc.dram_tensor("l1a_dtwT", [D, 1], F32, kind="ExternalInput").ap()
    wd["l1a_CwT"] = nc.dram_tensor("l1a_CwT", [16, D], F32, kind="ExternalInput").ap()
    wd["l1a_AAT"] = nc.dram_tensor("l1a_AAT", [32, 16], F32, kind="ExternalInput").ap()
    wd["l1a_cols"] = nc.dram_tensor("l1a_cols", [128, 84], F32, kind="ExternalInput").ap()
    wd["l1a_drow"] = nc.dram_tensor("l1a_drow", [16, 64], F32, kind="ExternalInput").ap()
    wd["l1a_ydram"] = nc.dram_tensor("l1a_ydram", [D, T], BF16, kind="Internal").ap()
    wd["l2f_router"] = nc.dram_tensor("l2f_router", [D, 7], F32, kind="ExternalInput").ap()
    wd["l2f_gT"] = [nc.dram_tensor(f"l2f_gT{e}", [D, 4096], F32, kind="ExternalInput").ap() for e in range(7)]
    wd["l2f_uT"] = [nc.dram_tensor(f"l2f_uT{e}", [D, 4096], F32, kind="ExternalInput").ap() for e in range(7)]
    wd["l2f_dT"] = [nc.dram_tensor(f"l2f_dT{e}", [4096, D], F32, kind="ExternalInput").ap() for e in range(7)]
    wd["l2f_shgT"] = nc.dram_tensor("l2f_shgT", [D, 4096], F32, kind="ExternalInput").ap()
    wd["l2f_shuT"] = nc.dram_tensor("l2f_shuT", [D, 4096], F32, kind="ExternalInput").ap()
    wd["l2f_shdT"] = nc.dram_tensor("l2f_shdT", [4096, D], F32, kind="ExternalInput").ap()
    mask_in = nc.dram_tensor("l2a_mask", [128, 128], F32, kind="ExternalInput").ap()
    bsel_in = nc.dram_tensor("l2a_bsel", [128, 128], F32, kind="ExternalInput").ap()
    contrib = nc.dram_tensor("l2a_contrib", [T, D], F32, kind="Internal").ap()
    gath = nc.dram_tensor("l2a_gath", [n_cores * T, D], F32, kind="Internal").ap()
    gath2 = nc.dram_tensor("l2a_gath2", [n_cores * T, D], F32, kind="Internal").ap()
    myout = nc.dram_tensor("l2a_myout", [T, D], F32, kind="Internal").ap()
    wd["l3f_gT"] = nc.dram_tensor("l3f_gT", [D, 3072], I32, kind="ExternalInput").ap()
    wd["l3f_uT"] = nc.dram_tensor("l3f_uT", [D, 3072], I32, kind="ExternalInput").ap()
    wd["l3f_dT"] = nc.dram_tensor("l3f_dT", [3072, D], I32, kind="ExternalInput").ap()
    for nm in ("wq", "wk", "wv", "wo"):
        wd[f"l3a_{nm}"] = nc.dram_tensor(f"l3a_{nm}", [D, D], F32, kind="ExternalInput").ap()

    with ExitStack() as ctx:
        P = Prog(_UniqNC(nc), ctx)
        sy = P.eng["sync"]
        ins = sy.dma_start(out=xout[:, :], in_=xin[:, :])
        P.sX.inc(ins, 16)
        sy.wait_ge(P.sX.h, P.sX.n)

        def ffn_phase(which):
            with ExitStack() as bctx:
                sc = alloc_ffn_scratch(P, bctx, min(T, 2048))
                sl = []
                if which == "l1f":
                    sl.append((sc["ebrow"][0:1, 0:16], wd["l1f_ebias"][0:1, :]))
                ritems = {"l0f": [(wd["l0f_router"], "rw_l0", 8)],
                          "l1f": [(wd["l1f_router"], "rw_l1", 16)],
                          "l2f": [(wd["l2f_router"], "rw_l2", 7)],
                          "l3f": []}[which]
                common_init(P, sc, ident_in, sl, ritems)
                emit_ffn_block(P, which, xout, wd, T, sc)
                sy.wait_ge(P.sO.h, P.sO.n)
                P.phase_barrier()

        def attn_phase(which):
            with ExitStack() as bctx:
                if which == "l1a":
                    sc = alloc_l1a_scratch(P, bctx, T)
                    common_init(P, sc, ident_in, [], ())
                    emit_l1attn_block(P, xout, wd, T, sc)
                elif which == "l2a":
                    sc = alloc_l2a_scratch(P, bctx, T)
                    common_init(P, sc, ident_in, [], ())
                    emit_l2attn_block(P, xout, T, sc, contrib, gath, gath2, myout,
                                      mask_in, bsel_in, n_cores)
                else:
                    sc = alloc_attn_scratch(P, bctx, T)
                    common_init(P, sc, ident_in, [], ())
                    emit_attn_block(P, which, xout, wd, T, sc)
                sy.wait_ge(P.sO.h, P.sO.n)
                P.phase_barrier()

        for li, aw in ((0, "l0a"), (1, "l1a"), (2, "l2a"), (3, "l3a")):
            ffn_phase(f"l{li}f")
            P.mark_phase(f"l{li}f_pre")
            attn_phase(aw)
            P.mark_phase(aw)
            ffn_phase(f"l{li}f")
            P.mark_phase(f"l{li}f_post")

        block = ctx.enter_context(nc.Block())
        P.finalize(block)
    return nc


_PROG_CACHE = {}


def _prog(key, builder):
    if key not in _PROG_CACHE:
        _PROG_CACHE[key] = builder()
    return _PROG_CACHE[key]


def kernel(**inputs):
    T, B = 4096, 8
    I = np.eye(128, dtype=np.float32)
    g = lambda k: np.asarray(inputs[k])

    w_l0f = dict(l0f_router=hC(g("L0_ffn_router")),
                 l0f_shgT=hC(g("L0_ffn_shared_gate")), l0f_shuT=hC(g("L0_ffn_shared_up")),
                 l0f_shdT=hC(g("L0_ffn_shared_down")))
    for e in range(8):
        w_l0f[f"l0f_gT{e}"] = hC(g("L0_ffn_experts_gate")[e])
        w_l0f[f"l0f_uT{e}"] = hC(g("L0_ffn_experts_up")[e])
        w_l0f[f"l0f_dT{e}"] = hC(g("L0_ffn_experts_down")[e])
    w_l0a = dict(l0a_wqkv=hC(g("L0_attn_w_qkv")), l0a_wproj=hC(g("L0_attn_w_proj")),
                 l0a_bias=np.concatenate([g("L0_attn_b_qkv"), g("L0_attn_b_proj")])[None, :]
                 .astype(np.float32))
    w_l1f = dict(l1f_router=hT(g("L1_ffn_router_weight")),
                 l1f_ebias=hC(g("L1_ffn_e_bias"))[None, :],
                 l1f_shgT=hT(g("L1_ffn_sh_gate_w")), l1f_shuT=hT(g("L1_ffn_sh_up_w")),
                 l1f_shdT=hT(g("L1_ffn_sh_down_w")))
    for e in range(16):
        w_l1f[f"l1f_gT{e}"] = hT(g("L1_ffn_gate_w")[e])
        w_l1f[f"l1f_uT{e}"] = hT(g("L1_ffn_up_w")[e])
        w_l1f[f"l1f_dT{e}"] = hT(g("L1_ffn_down_w")[e])
    w_l1a = dict(l1a_wkT=hT(g("L1_attn_wk")), l1a_wvT=hT(g("L1_attn_wv")),
                 l1a_wrT=hT(g("L1_attn_wr")), l1a_woT=hT(g("L1_attn_wo")),
                 l1a_BwT=hT(g("L1_attn_B_w")), l1a_dtwT=hT(g("L1_attn_dt_w")),
                 l1a_CwT=hT(g("L1_attn_C_w")),
                 l1a_AAT=np.concatenate([g("L1_attn_A"), hT(g("L1_attn_A"))], axis=0),
                 l1a_cols=pack_l1a_cols(g("L1_attn_time_decay"), g("L1_attn_time_first"),
                                        g("L1_attn_D_vec"), g("L1_attn_g_ssm"), g("L1_attn_b_ssm"),
                                        g("L1_attn_g_rwkv"), g("L1_attn_b_rwkv"),
                                        g("L1_attn_g_out"), g("L1_attn_b_out"),
                                        g("L1_attn_sn_g"), g("L1_attn_sn_b"), g("L1_attn_dt_b")),
                 l1a_drow=hC(g("L1_attn_time_decay")))
    w_l2f = dict(l2f_router=hT(g("L2_ffn_router_w")), l2f_shgT=hT(g("L2_ffn_shared_gate")),
                 l2f_shuT=hT(g("L2_ffn_shared_up")), l2f_shdT=hT(g("L2_ffn_shared_down")))
    for e in range(7):
        w_l2f[f"l2f_gT{e}"] = hT(g("L2_ffn_routed_gate")[e])
        w_l2f[f"l2f_uT{e}"] = hT(g("L2_ffn_routed_up")[e])
        w_l2f[f"l2f_dT{e}"] = hT(g("L2_ffn_routed_down")[e])
    w_l3a = dict(l3a_wq=hC(g("L3_attn_Wq")), l3a_wk=hC(g("L3_attn_Wk")),
                 l3a_wv=hC(g("L3_attn_Wv")), l3a_wo=hC(g("L3_attn_Wo")))
    w_l3f = dict(l3f_gT=hT(g("L3_ffn_w_gate")), l3f_uT=hT(g("L3_ffn_w_up")),
                 l3f_dT=hT(g("L3_ffn_w_down")))

    PS = 128 // B
    l2a_mask = np.full((128, 128), -1e31, np.float32)
    for r in range(128):
        for c in range(128):
            if r % PS == c % PS:
                l2a_mask[r, c] = 0.0
    l2a_bsel = []
    for c in range(B):
        bs = np.zeros((128, 128), np.float32)
        for q in range(PS):
            bs[c * PS + q, q] = 1.0
        l2a_bsel.append(bs)

    xcur = [np.ascontiguousarray(g("L0_attn_x")[b].T) for b in range(B)]

    wall = {}
    for wdict in (w_l0f, w_l0a, w_l1f, w_l1a, w_l2f, w_l3a, w_l3f):
        wall.update(wdict)
    wall["l2a_mask"] = l2a_mask

    ncf = _prog("fused", lambda: build_fused_program(T, B))
    in_maps = [dict(wall, xin=xcur[b], ident_in=I, l2a_bsel=l2a_bsel[b])
               for b in range(B)]
    res = run_bass_kernel_spmd(ncf, in_maps, core_ids=list(range(B)))
    return np.stack([np.asarray(res.results[b]["xout"]).T for b in range(B)],
                    axis=0).astype(np.float32)
